# version 22
# baseline (speedup 1.0000x reference)
import contextlib
import numpy as np
import ml_dtypes

import concourse.bass as bass
import concourse.mybir as mybir
from concourse.bass_utils import run_bass_kernel_spmd

BF = ml_dtypes.bfloat16
F32 = mybir.dt.float32
BF16 = mybir.dt.bfloat16
AF = mybir.ActivationFunctionType
ALU = mybir.AluOpType
AX = mybir.AxisListType

N = 2048
D = 1024
NB = 16
QBS = 4
KC = 16
NE = 8
SCALE = 32.0 ** -0.5
EPS0 = 1e-5
EPS1 = 1e-6



def build_nc():
    nc = bass.Bass("TRN2", target_bir_lowering=False, debug=False)

    x_d = nc.dram_tensor("x", [N, D], F32, kind="ExternalInput").ap()
    wqk_d = nc.dram_tensor("wqk", [8, 8, 128, 256], F32, kind="ExternalInput").ap()
    wv_d = nc.dram_tensor("wv", [4, 8, 128, 256], F32, kind="ExternalInput").ap()
    wout_d = nc.dram_tensor("wout", [2, 8, 128, 512], F32, kind="ExternalInput").ap()
    wet_d = nc.dram_tensor("wet", [NE, 2, 8, 128, 512], F32, kind="ExternalInput").ap()
    wg_d = nc.dram_tensor("wg", [8, 128, 8], F32, kind="ExternalInput").ap()
    bout_d = nc.dram_tensor("bout", [1, D], F32, kind="ExternalInput").ap()
    bg_d = nc.dram_tensor("bg", [1, 8], F32, kind="ExternalInput").ap()
    be_d = nc.dram_tensor("be", [8, D], F32, kind="ExternalInput").ap()
    gam_d = nc.dram_tensor("gam", [1, D], F32, kind="ExternalInput").ap()
    bet_d = nc.dram_tensor("bet", [1, D], F32, kind="ExternalInput").ap()
    ident_d = nc.dram_tensor("ident_hbm", [128, 128], BF16, kind="ExternalInput").ap()
    blko_d = nc.dram_tensor("blko_hbm", [128, 4], BF16, kind="ExternalInput").ap()
    out_d = nc.dram_tensor("out", [N, D], F32, kind="ExternalOutput").ap()
    ffn_d = nc.dram_tensor("ffn_scratch", [2, 2, 128, 8, 512], F32, kind="Internal").ap()

    A = lambda name, shape, dt: nc.alloc_sbuf_tensor(name, shape, dt)
    ident = A("ident", [128, 128], BF16)
    blko = A("blko", [128, 4], BF16)
    rows = A("rows", [128, D], F32)
    be_bf = A("be_bf", [8, D], BF16)
    ones_col = A("ones_col", [128, 128], F32)
    xn_tok = A("xn_tok", [128, 2, D], BF16)
    stats = A("stats", [128, 24], F32)
    mv = A("mv", [128, 4], F32)
    scr = A("scr", [128, 24], F32)
    ln1r = A("ln1r", [128, NB, 2], F32)
    gsm = A("gsm", [128, 33], F32)
    gp = A("gp", [128, 8], F32)
    gmask = A("gmask", [128, 8], F32)
    ghat = A("ghat", [128, NB, 8], F32)
    ghat_bf = A("ghat_bf", [128, 8], BF16)
    ghatT = A("ghatT", [8, N], BF16)
    nmax = A("nmax", [128, 13], F32)
    nmax_r = A("nmax_r", [1, 8], F32)
    dmy = A("dmy", [1, 64], F32)
    cs = A("cs", [1, 4], F32)
    negc_bc = A("negc_bc", [128, 2], F32)
    qkT = A("qkT", [128, 2, 2, N], BF16)
    ones_bf = A("ones_bf", [128, 1], BF16)
    wgb = A("wgb", [128, 8, 8], BF16)
    sq = xn_tok.ap().rearrange("p a b -> p (a b)")

    used_A = nc.bytes_allocated(ident.ap().space)
    assert used_A <= 55296, f"region A overflow: {used_A}"

    OFF_B = used_A
    xnT = nc.alloc_sbuf_tensor_at("xnT", [128, 8, N], BF16, offset=OFF_B)
    OFF_C = OFF_B + 32768
    attT = nc.alloc_sbuf_tensor_at("attT", [128, 8, N], BF16, offset=OFF_C)
    xblk = nc.alloc_sbuf_tensor_at("xblk", [128, 2, D], F32, offset=OFF_C)
    outb = nc.alloc_sbuf_tensor_at("outb", [128, 4, 512], F32, offset=OFF_C + 8192)
    asmA = nc.alloc_sbuf_tensor_at("asmA", [128, 2, 512], F32, offset=OFF_C + 16384)
    asmB = nc.alloc_sbuf_tensor_at("asmB", [128, 2, 512], F32, offset=OFF_C + 20480)
    OFF_D = OFF_C + 32768
    vaug = nc.alloc_sbuf_tensor_at("vaug", [128, 2, KC, 8, 32], BF16, offset=OFF_D)
    pT = nc.alloc_sbuf_tensor_at("pT", [128, 2, 2048], BF16, offset=OFF_D + 16384)
    zrec = nc.alloc_sbuf_tensor_at("zrec", [128, 512], F32, offset=OFF_D + 24576)
    av_m = nc.alloc_sbuf_tensor_at("av_m", [128, 2, 512], F32, offset=OFF_D + 26624)
    projbf = nc.alloc_sbuf_tensor_at("projbf", [128, NB, D], BF16, offset=OFF_D)
    OFF_E = OFF_D + 33792
    staging = nc.alloc_sbuf_tensor_at("staging", [128, 2048], F32, offset=OFF_E)
    wstream = nc.alloc_sbuf_tensor_at("wstream", [128, 2, 8, 256], BF16, offset=OFF_E + 8192)
    webb = nc.alloc_sbuf_tensor_at("webb", [128, 2, 8, 512], BF16, offset=OFF_E + 16384)
    ffn_q = nc.alloc_sbuf_tensor_at("ffn_q", [128, 8, 512], F32, offset=OFF_E + 32768)
    gbc = nc.alloc_sbuf_tensor_at("gbc", [128, D], F32, offset=OFF_E + 49152)
    bbc = nc.alloc_sbuf_tensor_at("bbc", [128, D], F32, offset=OFF_E + 53248)
    woutb = nc.alloc_sbuf_tensor_at("woutb", [128, 2, 8, 512], BF16, offset=OFF_E + 8192)
    ffn_rel = nc.alloc_sbuf_tensor_at("ffn_rel", [128, 4, 512], F32, offset=OFF_E + 24576)
    assert OFF_E + 57344 <= 207168, OFF_E

    arena = nc.alloc_psum_tensor("arena", [128, 4096], F32)
    sc_ps = arena.ap()[:, 0:2048]
    av_ps = arena.ap()[:, 2048:2560]
    z_ps = arena.ap()[:, 2560:3072]
    fz_ps = [arena.ap()[:, 3072:3584], arena.ap()[:, 3584:4096]]
    mm_ps = [arena.ap()[:, 0:512], arena.ap()[:, 512:1024]]
    tp_ps = [arena.ap()[:, 3072:3136].bitcast(BF16), arena.ap()[:, 3584:3648].bitcast(BF16)]

    es = contextlib.ExitStack()

    class Ctr:
        def __init__(self, name):
            self.h = es.enter_context(nc.semaphore(name))
            self.v = 0

        def inc(self, k=1):
            self.v += k
            return self.v

    s_ld = Ctr("s_ld")
    s_c = Ctr("s_c")
    s_x0 = Ctr("s_x0")
    s_x1 = Ctr("s_x1")
    s_gb = Ctr("s_gb")
    s_f0 = Ctr("s_f0")
    s_f1 = Ctr("s_f1")
    s_nx = Ctr("s_nx")
    s_fl = Ctr("s_fl")
    s_st = Ctr("s_st")
    s_s1 = Ctr("s_s1")
    s_mv = Ctr("s_mv")
    s_cast = Ctr("s_cast")
    s_vln = Ctr("s_vln")
    s_v2s = Ctr("s_v2s")
    s_s2v = Ctr("s_s2v")
    s_tp = Ctr("s_tp")
    s_tpe = Ctr("s_tpe")
    s_mm = Ctr("s_mm")
    s_me = Ctr("s_me")
    s_sc = Ctr("s_sc")
    s_xp = Ctr("s_xp")
    s_avc = Ctr("s_avc")
    s_zr = Ctr("s_zr")
    s_bc = Ctr("s_bc")
    s_nm = Ctr("s_nm")
    s_nmb = Ctr("s_nmb")
    s_fm = Ctr("s_fm")
    s_fe = Ctr("s_fe")
    s_as = Ctr("s_as")

    ops = {k: [] for k in ("sync", "tensor", "scalar", "vector", "gpsimd")}

    def em(eng, f):
        ops[eng].append(f)

    def load(dst_ap, src_ap, ch=None):
        ch = ch if ch is not None else s_ld
        v = ch.inc(16)
        em("sync", lambda q, d=dst_ap, s=src_ap, ch=ch: q.dma_start(out=d, in_=s).then_inc(ch.h, 16))
        return v

    def wait(eng, ctr, val):
        if val > 0:
            em(eng, lambda e, c=ctr, v=val: e.wait_ge(c.h, v))

    def staging_guard():
        wait("sync", s_cast, s_cast.v)

    import collections
    fq_pri = collections.deque()
    fq_bulk = collections.deque()
    fst = {"idx": 0, "popped": 0}

    def claim_slots(n):
        j = fst["idx"]
        fst["idx"] += n
        if n == 1:
            if j >= 2:
                wait("tensor", s_fe, j - 1)
            return j % 2
        else:
            if j >= 1:
                wait("tensor", s_fe, j)
            return None

    PF_LOOKAHEAD = 8

    def pop_one():
        q = fq_pri if fq_pri else fq_bulk
        if not q:
            return False
        job = q.popleft()
        fst["popped"] += 1
        seen = 0
        for qq in (fq_pri, fq_bulk):
            for jb in qq:
                if seen >= PF_LOOKAHEAD:
                    break
                lc = jb.get("lc")
                if lc is not None and not jb.get("lc_done"):
                    lc()
                    jb["lc_done"] = True
                seen += 1
            if seen >= PF_LOOKAHEAD:
                break
        if job.get("lc") is not None and not job.get("lc_done"):
            job["lc"]()
            job["lc_done"] = True
        job["run"]()
        return True

    def pop_fillers(n):
        for _ in range(n):
            if not pop_one():
                return

    load(ident.ap(), ident_d, s_c)
    load(blko.ap(), blko_d, s_c)
    load(rows.ap()[32:33, :], bout_d, s_c)
    load(rows.ap()[64:65, 0:8], bg_d, s_c)
    v_consts = load(rows.ap()[0:8, :], be_d, s_c)
    wait("tensor", s_c, v_consts)
    wait("vector", s_c, v_consts)

    em("vector", lambda v: v.memset(ones_col.ap(), 1.0))
    em("vector", lambda v: v.memset(ones_bf.ap(), 1.0))
    em("vector", lambda v: v.memset(z_ps, 0.0))
    em("vector", lambda v: v.drain())
    v_setup = s_vln.inc()
    em("vector", lambda v: v.tensor_copy(out=ones_col.ap()[:, 0:1], in_=ones_col.ap()[:, 0:1]).then_inc(s_vln.h, 1))

    wait("gpsimd", s_c, v_consts)
    c_bebf = s_cast.inc()
    em("gpsimd", lambda g: g.tensor_copy(out=be_bf.ap(), in_=rows.ap()[0:8, :]).then_inc(s_cast.h, 1))

    def ln_chain(eps, r_dst, negmur_dst, p=0):
        o = 8 * p
        v2s = s_v2s.inc()
        em("vector", lambda v, eps=eps, o=o, p=p: v.tensor_scalar(
            out=scr.ap()[:, o:o + 1], in0=mv.ap()[:, 2 * p + 1:2 * p + 2], scalar1=eps,
            scalar2=None, op0=ALU.add).then_inc(s_v2s.h, 1))
        wait("scalar", s_v2s, v2s)
        em("scalar", lambda s, o=o: s.activation(out=scr.ap()[:, o + 1:o + 2], in_=scr.ap()[:, o:o + 1], func=AF.Ln))
        em("scalar", lambda s: s.drain())
        s2v = s_s2v.inc()
        em("scalar", lambda s, o=o: s.activation(out=scr.ap()[:, o + 2:o + 3], in_=scr.ap()[:, o + 1:o + 2],
                                                 func=AF.Exp, scale=-0.5).then_inc(s_s2v.h, 1))
        wait("vector", s_s2v, s2v)
        if r_dst is not None:
            em("vector", lambda v, o=o: v.tensor_copy(out=r_dst, in_=scr.ap()[:, o + 2:o + 3]))
            em("vector", lambda v: v.drain())
        em("vector", lambda v, o=o, p=p: v.tensor_tensor(out=scr.ap()[:, o + 3:o + 4], in0=mv.ap()[:, 2 * p:2 * p + 1],
                                                         in1=scr.ap()[:, o + 2:o + 3], op=ALU.mult))
        em("vector", lambda v: v.drain())
        em("vector", lambda v, d=negmur_dst, o=o: v.tensor_scalar(
            out=d, in0=scr.ap()[:, o + 3:o + 4], scalar1=-1.0, scalar2=None, op0=ALU.mult))
        em("vector", lambda v: v.drain())

    staging_guard()
    v_wg = load(staging.ap()[:, 0:64], wg_d.rearrange("ic p c -> p ic c"))
    wait("gpsimd", s_ld, v_wg)
    c_wg = s_cast.inc()
    em("gpsimd", lambda g: g.tensor_copy(
        out=wgb.ap(), in_=staging.ap()[:, 0:64].rearrange("p (ic c) -> p ic c", ic=8)).then_inc(s_cast.h, 1))

    def gate_nb(nb):
        bank = nb % 2
        if nb == 0:
            wait("tensor", s_cast, c_wg)
        wait("tensor", s_me, s_me.v - 1)
        for ic in range(8):
            em("tensor", lambda t, nb=nb, ic=ic, bank=bank: t.matmul(
                mm_ps[bank][0:128, 0:8], xnT.ap()[:, ic, nb * 128:(nb + 1) * 128],
                wgb.ap()[:, ic, :], start=(ic == 0), stop=False))
        mmv = s_mm.inc()
        em("tensor", lambda t, bank=bank: t.matmul(
            mm_ps[bank][0:128, 0:8], ones_col.ap()[64:65, 0:128], rows.ap()[64:65, 0:8],
            start=False, stop=True, tile_position=(64, 0)).then_inc(s_mm.h, 1))
        wait("vector", s_mm, mmv)
        em("vector", lambda v, bank=bank: v.reduce_max(out=gsm.ap()[:, 0:1],
                                                       in_=mm_ps[bank][0:128, 0:8], axis=AX.X))
        em("vector", lambda v: v.drain())
        v2s = s_v2s.inc()
        em("vector", lambda v: v.tensor_scalar(out=gsm.ap()[:, 1:2], in0=gsm.ap()[:, 0:1], scalar1=-1.0,
                                               scalar2=None, op0=ALU.mult).then_inc(s_v2s.h, 1))
        wait("scalar", s_v2s, v2s)
        s2v = s_s2v.inc()
        em("scalar", lambda s, bank=bank: s.activation(out=gp.ap(), in_=mm_ps[bank][0:128, 0:8],
                                                       func=AF.Exp, bias=gsm.ap()[:, 1:2]).then_inc(s_s2v.h, 1))
        s_me.inc()
        em("scalar", lambda s, nb=nb: s.copy(out=gsm.ap()[:, 17 + nb:18 + nb],
                                             in_=ones_col.ap()[0:128, 0:1]).then_inc(s_me.h, 1))
        wait("vector", s_s2v, s2v)
        em("vector", lambda v: v.reduce_sum(out=gsm.ap()[:, 2:3], in_=gp.ap(), axis=AX.X))
        em("vector", lambda v: v.drain())
        em("vector", lambda v: v.reciprocal(out=gsm.ap()[:, 3:4], in_=gsm.ap()[:, 2:3]))
        em("vector", lambda v: v.drain())
        em("vector", lambda v: v.tensor_scalar(out=gp.ap(), in0=gp.ap(), scalar1=gsm.ap()[:, 3:4],
                                               scalar2=None, op0=ALU.mult))
        em("vector", lambda v: v.drain())
        em("vector", lambda v: v.reduce_max(out=gsm.ap()[:, 4:5], in_=gp.ap(), axis=AX.X))
        em("vector", lambda v: v.drain())
        em("vector", lambda v: v.tensor_scalar(out=gmask.ap(), in0=gp.ap(), scalar1=gsm.ap()[:, 4:5],
                                               scalar2=None, op0=ALU.is_equal))
        em("vector", lambda v: v.drain())
        em("vector", lambda v: v.scalar_tensor_tensor(out=gsm.ap()[:, 8:16], in0=gmask.ap(), scalar=-1e30,
                                                      in1=gp.ap(), op0=ALU.mult, op1=ALU.add))
        em("vector", lambda v: v.drain())
        em("vector", lambda v: v.reduce_max(out=gsm.ap()[:, 5:6], in_=gsm.ap()[:, 8:16], axis=AX.X))
        em("vector", lambda v: v.drain())
        em("vector", lambda v: v.tensor_scalar(out=gsm.ap()[:, 8:16], in0=gp.ap(), scalar1=gsm.ap()[:, 5:6],
                                               scalar2=None, op0=ALU.is_equal))
        em("vector", lambda v: v.drain())
        em("vector", lambda v: v.tensor_add(out=gmask.ap(), in0=gmask.ap(), in1=gsm.ap()[:, 8:16]))
        em("vector", lambda v: v.drain())
        em("vector", lambda v: v.tensor_mul(out=gmask.ap(), in0=gmask.ap(), in1=gp.ap()))
        em("vector", lambda v: v.tensor_add(out=gsm.ap()[:, 6:7], in0=gsm.ap()[:, 4:5], in1=gsm.ap()[:, 5:6]))
        em("vector", lambda v: v.drain())
        em("vector", lambda v: v.reciprocal(out=gsm.ap()[:, 7:8], in_=gsm.ap()[:, 6:7]))
        em("vector", lambda v: v.drain())
        em("vector", lambda v, nb=nb: v.tensor_scalar(out=ghat.ap()[:, nb, :], in0=gmask.ap(),
                                                      scalar1=gsm.ap()[:, 7:8], scalar2=None, op0=ALU.mult))
        em("vector", lambda v: v.drain())
        vln = s_vln.inc()
        em("vector", lambda v, nb=nb: v.tensor_copy(out=ghat_bf.ap(), in_=ghat.ap()[:, nb, :]).then_inc(s_vln.h, 1))
        wait("tensor", s_vln, vln)
        tpar = nb % 2
        wait("tensor", s_tpe, s_tpe.v - 1)
        tv = s_tp.inc()
        em("tensor", lambda t, tpar=tpar: t.transpose(
            out=tp_ps[tpar][0:8, 0:128], in_=ghat_bf.ap(), identity=ident.ap()).then_inc(s_tp.h, 1))
        wait("vector", s_tp, tv)
        s_tpe.inc()
        em("vector", lambda v, nb=nb, tpar=tpar: v.tensor_copy(
            out=ghatT.ap()[0:8, nb * 128:(nb + 1) * 128], in_=tp_ps[tpar][0:8, 0:128]).then_inc(s_tpe.h, 1))
    v_gate_done = None

    ln0_tp = {}
    ln0_xn = {}
    for nb0 in range(0, NB, 2):
        pair = (nb0, nb0 + 1)
        for nb in pair:
            par = nb % 2
            if nb >= 2:
                wait("sync", s_vln, ln0_xn[nb - 2])
            v_x = load(xblk.ap()[:, par, :], x_d[nb * 128:(nb + 1) * 128, :], (s_x0, s_x1)[par])
        wait("vector", s_x0, s_x0.v)
        wait("vector", s_x1, s_x1.v)
        for nb in pair:
            p = nb % 2
            em("vector", lambda v, p=p: v.bn_stats(out=stats.ap()[:, 12 * p:12 * p + 6],
                                                   in_=xblk.ap()[:, p, 0:512]))
            em("vector", lambda v, p=p: v.bn_stats(out=stats.ap()[:, 12 * p + 6:12 * p + 12],
                                                   in_=xblk.ap()[:, p, 512:1024]))
        em("vector", lambda v: v.drain())
        for nb in pair:
            p = nb % 2
            em("vector", lambda v, p=p: v.bn_aggr(out=mv.ap()[:, 2 * p:2 * p + 2],
                                                  in_=stats.ap()[:, 12 * p:12 * p + 12]))
        em("vector", lambda v: v.drain())
        for nb in pair:
            p = nb % 2
            ln_chain(EPS0, None, scr.ap()[:, 8 * p + 4:8 * p + 5], p)
        if nb0 >= 2:
            wait("vector", s_tp, ln0_tp[nb0 - 1])
        for nb in pair:
            p = nb % 2
            em("vector", lambda v, p=p: v.tensor_scalar(
                out=xn_tok.ap()[:, p, :], in0=xblk.ap()[:, p, :], scalar1=scr.ap()[:, 8 * p + 2:8 * p + 3],
                scalar2=scr.ap()[:, 8 * p + 4:8 * p + 5], op0=ALU.mult, op1=ALU.add).then_inc(s_vln.h, 1))
            vln = s_vln.inc()
            ln0_xn[nb] = vln
        wait("tensor", s_vln, s_vln.v)
        for fc in range(8):
            for nb in pair:
                p = nb % 2
                tpar = s_tp.v % 2
                if s_tp.v >= 2:
                    wait("tensor", s_tpe, s_tpe.v - 1)
                tv = s_tp.inc()
                em("tensor", lambda t, p=p, fc=fc, tpar=tpar: t.transpose(
                    out=tp_ps[tpar], in_=xn_tok.ap()[:, p, fc * 128:(fc + 1) * 128],
                    identity=ident.ap()).then_inc(s_tp.h, 1))
                wait("vector", s_tp, tv)
                s_tpe.inc()
                em("vector", lambda v, nb=nb, fc=fc, tpar=tpar: v.tensor_copy(
                    out=xnT.ap()[:, fc, nb * 128:(nb + 1) * 128], in_=tp_ps[tpar]).then_inc(s_tpe.h, 1))
        for nb in pair:
            ln0_tp[nb] = s_tp.v
        if nb0 >= 2:
            gate_nb(nb0 - 2)
            gate_nb(nb0 - 1)
        if nb0 == NB - 2:
            gate_nb(nb0)
            gate_nb(nb0 + 1)

    v_gate_done = s_tpe.v
    wait("tensor", s_tpe, v_gate_done)

    wchunk = {"seq": 0}
    wchunk_done = {}

    def make_stream_chunk(kind, key, src_chunks, dst_slot_ap_fn):
        seq = wchunk["seq"]
        wchunk["seq"] += 1
        par = seq % 2
        state = {}

        def emit_loadcast():
            staging_guard()
            v_w = 0
            for i, sc in enumerate(src_chunks):
                v_w = load(staging.ap()[:, i * 256:(i + 1) * 256], sc)
            wait("vector", s_ld, v_w)
            if seq >= 2:
                assert (seq - 2) in wchunk_done, f"stream chunk {seq}: prefetch too deep"
                wait("vector", s_fe, wchunk_done[seq - 2])
            cv = s_cast.inc()
            em("vector", lambda v, par=par: v.tensor_copy(
                out=dst_slot_ap_fn(par),
                in_=staging.ap().rearrange("p (i c) -> p i c", i=8)).then_inc(s_cast.h, 1))
            state["cast"] = cv
        return seq, par, emit_loadcast, state

    def qk_jobs(hg):
        qpar = hg % 2
        seq, wpar, emit_loadcast, lcstate = make_stream_chunk(
            "qk", hg, [wqk_d[hg, fc] for fc in range(8)],
            lambda par: wstream.ap()[:, par, :, :])
        state = {}
        jobs = []
        for grp in range(8):
            cb, nbq = grp // 4, grp % 4

            def job(grp=grp, cb=cb, nbq=nbq, qpar=qpar, wpar=wpar, seq=seq):
                slot = claim_slots(1)
                wait("tensor", s_cast, lcstate["cast"])
                fv = s_fm.inc()
                for fc in range(8):
                    last = fc == 7
                    em("tensor", lambda t, fc=fc, cb=cb, nbq=nbq, slot=slot, qpar=qpar, wpar=wpar, last=last: t.matmul(
                        fz_ps[slot][0:128, 0:512],
                        wstream.ap()[:, wpar, fc, cb * 128:(cb + 1) * 128],
                        xnT.ap()[:, fc, nbq * 512:(nbq + 1) * 512],
                        start=(fc == 0), stop=last).then_inc(s_fm.h, 1) if last else t.matmul(
                        fz_ps[slot][0:128, 0:512],
                        wstream.ap()[:, wpar, fc, cb * 128:(cb + 1) * 128],
                        xnT.ap()[:, fc, nbq * 512:(nbq + 1) * 512],
                        start=(fc == 0), stop=last))
                wait("vector", s_fm, fv)
                s_fe.inc()
                em("vector", lambda v, cb=cb, nbq=nbq, slot=slot, qpar=qpar: v.tensor_copy(
                    out=qkT.ap()[:, qpar, cb, nbq * 512:(nbq + 1) * 512],
                    in_=fz_ps[slot][0:128, 0:512]).then_inc(s_fe.h, 1))
                if grp == 7:
                    state["done"] = s_fe.v
                    wchunk_done[seq] = s_fe.v
            jobs.append({"run": job, "lc": emit_loadcast if grp == 0 else None})
        return jobs, state

    def v_jobs(W):
        wpar_v = W % 2
        seq, wpar, emit_loadcast, lcstate = make_stream_chunk(
            "wv", W, [wv_d[W, ic] for ic in range(8)],
            lambda par: wstream.ap()[:, par, :, :])
        state = {}
        jobs = []
        for nb in range(NB):
            def job(nb=nb, wpar_v=wpar_v, wpar=wpar, seq=seq):
                slot = claim_slots(1)
                wait("tensor", s_cast, lcstate["cast"])
                fv = s_fm.inc()
                for ic in range(8):
                    last = ic == 7
                    em("tensor", lambda t, ic=ic, nb=nb, slot=slot, wpar=wpar, last=last: t.matmul(
                        fz_ps[slot][0:128, 0:256],
                        xnT.ap()[:, ic, nb * 128:(nb + 1) * 128],
                        wstream.ap()[:, wpar, ic, :],
                        start=(ic == 0), stop=last).then_inc(s_fm.h, 1) if last else t.matmul(
                        fz_ps[slot][0:128, 0:256],
                        xnT.ap()[:, ic, nb * 128:(nb + 1) * 128],
                        wstream.ap()[:, wpar, ic, :],
                        start=(ic == 0), stop=last))
                wait("vector", s_fm, fv)
                s_fe.inc()
                em("vector", lambda v, nb=nb, slot=slot, wpar_v=wpar_v: v.tensor_copy(
                    out=vaug.ap()[:, wpar_v, nb, :, :],
                    in_=fz_ps[slot][0:128, 0:256].rearrange("p (h d) -> p h d", h=8)).then_inc(s_fe.h, 1))
                if nb == NB - 1:
                    state["done"] = s_fe.v
                    wchunk_done[seq] = s_fe.v
            jobs.append({"run": job, "lc": emit_loadcast if nb == 0 else None})
        return jobs, state

    def c_job(hg, qk_state):
        state = {}
        sv = {}

        def sq_job(side):
            def f(side=side):
                qpar = hg % 2
                if side == 0:
                    wait("vector", s_fe, qk_state["done"])
                else:
                    wait("vector", s_mm, sv["mm0_last"])
                em("vector", lambda v, side=side, qpar=qpar: v.tensor_mul(
                    out=sq, in0=qkT.ap()[:, qpar, side, :], in1=qkT.ap()[:, qpar, side, :]))
                em("vector", lambda v: v.drain())
                vln = s_vln.inc()
                em("vector", lambda v: v.tensor_copy(out=sq[:, 0:1], in_=sq[:, 0:1]).then_inc(s_vln.h, 1))
                sv["sq" + str(side)] = vln
            return f

        def mm_job(side):
            def f(side=side):
                claim_slots(2)
                wait("tensor", s_vln, sv["sq" + str(side)])
                for nbq in range(4):
                    pslot = nbq % 2
                    mmv = s_mm.inc()
                    em("tensor", lambda t, nbq=nbq, pslot=pslot: t.matmul(
                        fz_ps[pslot][0:4, 0:512], blko.ap(), sq[:, nbq * 512:(nbq + 1) * 512],
                        start=True, stop=True).then_inc(s_mm.h, 1))
                    if nbq == 3 and side == 0:
                        sv["mm0_last"] = s_mm.v
                    wait("vector", s_mm, mmv)
                    mev = s_me.inc()
                    em("vector", lambda v, side=side, nbq=nbq, pslot=pslot: v.reduce_max(
                        out=nmax.ap()[0:4, side * 4 + nbq:side * 4 + nbq + 1],
                        in_=fz_ps[pslot][0:4, 0:512], axis=AX.X).then_inc(s_me.h, 1))
                    if nbq >= 1:
                        wait("tensor", s_me, mev - 1)
                wait("tensor", s_me, s_me.v)
                s_fe.inc(2)
                em("vector", lambda v, side=side: v.tensor_copy(
                    out=nmax.ap()[0:4, 11 + side:12 + side],
                    in_=ones_col.ap()[0:4, 0:1]).then_inc(s_fe.h, 2))
                if side == 1:
                    finish()
            return f

        def finish():
            em("vector", lambda v: v.drain())
            em("vector", lambda v: v.reduce_max(out=nmax.ap()[0:4, 8:9], in_=nmax.ap()[0:4, 0:4], axis=AX.X))
            em("vector", lambda v: v.reduce_max(out=nmax.ap()[0:4, 9:10], in_=nmax.ap()[0:4, 4:8], axis=AX.X))
            em("vector", lambda v: v.drain())
            vln = s_vln.inc()
            em("vector", lambda v: v.tensor_copy(out=nmax.ap()[0:4, 10:11],
                                                 in_=nmax.ap()[0:4, 8:9]).then_inc(s_vln.h, 1))
            wait("gpsimd", s_vln, vln)
            mvv = s_nx.inc(16)
            em("gpsimd", lambda g: g.dma_start(out=nmax_r.ap()[0:1, 0:8],
                                               in_=nmax.ap()[0:4, 8:10]).then_inc(s_nx.h, 16))
            wait("vector", s_nx, mvv)
            em("vector", lambda v: v.reduce_max(out=cs.ap()[0:1, 0:1], in_=nmax_r.ap()[0:1, 0:8:2], axis=AX.X))
            em("vector", lambda v: v.reduce_max(out=cs.ap()[0:1, 1:2], in_=nmax_r.ap()[0:1, 1:8:2], axis=AX.X))
            em("vector", lambda v: v.drain())
            em("vector", lambda v: v.tensor_tensor(out=cs.ap()[0:1, 2:3], in0=cs.ap()[0:1, 0:1],
                                                   in1=cs.ap()[0:1, 1:2], op=ALU.add))
            em("vector", lambda v: v.drain())
            vln = s_vln.inc()
            em("vector", lambda v: v.tensor_scalar(out=cs.ap()[0:1, 0:1], in0=cs.ap()[0:1, 2:3],
                                                   scalar1=-SCALE * 0.5, scalar2=None,
                                                   op0=ALU.mult).then_inc(s_vln.h, 1))
            state["negc_ready"] = s_vln.v

        def negc_job():
            slot = claim_slots(1)
            wait("tensor", s_vln, state["negc_ready"])
            mmv = s_mm.inc()
            em("tensor", lambda t, slot=slot: t.matmul(
                fz_ps[slot][0:128, 0:1], ones_col.ap()[0:1, 0:128], cs.ap()[0:1, 0:1],
                start=True, stop=True).then_inc(s_mm.h, 1))
            wait("vector", s_mm, mmv)
            s_fe.inc()
            em("vector", lambda v, hg=hg, slot=slot: v.tensor_copy(
                out=negc_bc.ap()[:, hg % 2:hg % 2 + 1], in_=fz_ps[slot][0:128, 0:1]).then_inc(s_fe.h, 1))
            state["done"] = s_fe.v
        return [sq_job(0), mm_job(0), sq_job(1), mm_job(1)], negc_job, state

    def moe_jobs():
        all_jobs = []
        gstate = {}
        for och in range(2):
            for nbh in range(2):
                for e in range(NE):
                    gkey = (och * 2 + nbh) * NE + e

                    def group_lc(och=och, e=e, gkey=gkey):
                        cvs = []
                        for rnd in range(2):
                            staging_guard()
                            v_w = 0
                            for ic4 in range(4):
                                ic = rnd * 4 + ic4
                                v_w = load(staging.ap()[:, ic4 * 512:(ic4 + 1) * 512], wet_d[e, och, ic])
                            wait("vector", s_ld, v_w)
                            if rnd == 0 and gkey >= 2:
                                assert "done" in gstate[gkey - 2], f"moe group {gkey}: prefetch too deep"
                                wait("vector", s_fe, gstate[gkey - 2]["done"])
                            if gkey == 0 and rnd == 0:
                                wait("vector", s_cast, c_bebf)
                            cv = s_cast.inc()
                            em("vector", lambda v, gkey=gkey, rnd=rnd: v.tensor_copy(
                                out=webb.ap()[:, gkey % 2, rnd * 4:(rnd + 1) * 4, :],
                                in_=staging.ap().rearrange("p (i c) -> p i c", i=4)).then_inc(s_cast.h, 1))
                            cvs.append(cv)
                        gstate[gkey]["cast0"] = cvs[0]
                        gstate[gkey]["cast"] = cvs[1]

                    gstate[gkey] = {}
                    for nbl in range(8):
                        nb = nbh * 8 + nbl

                        def job(och=och, nbh=nbh, e=e, nb=nb, nbl=nbl, gkey=gkey):
                            slot = claim_slots(1)
                            wait("tensor", s_cast, gstate[gkey]["cast0"])
                            if e == 0 and nbl == 0 and gkey >= NE:
                                wait("vector", s_fl, gstate["flush" + str(gkey // NE - 1)])
                            fv = s_fm.inc()
                            n_ic = 8 if e == 0 else 7
                            for ic in range(n_ic):
                                if ic == 4:
                                    wait("tensor", s_cast, gstate[gkey]["cast"])
                                em("tensor", lambda t, nb=nb, ic=ic, gkey=gkey, slot=slot: t.matmul(
                                    fz_ps[slot][0:128, 0:512], xnT.ap()[:, ic, nb * 128:(nb + 1) * 128],
                                    webb.ap()[:, gkey % 2, ic, :], start=(ic == 0), stop=False))
                            if e == 0:
                                em("tensor", lambda t, nb=nb, och=och, slot=slot: t.matmul(
                                    fz_ps[slot][0:128, 0:512], ghatT.ap()[0:8, nb * 128:(nb + 1) * 128],
                                    be_bf.ap()[0:8, och * 512:(och + 1) * 512],
                                    start=False, stop=True).then_inc(s_fm.h, 1))
                            else:
                                em("tensor", lambda t, nb=nb, gkey=gkey, slot=slot: t.matmul(
                                    fz_ps[slot][0:128, 0:512], xnT.ap()[:, 7, nb * 128:(nb + 1) * 128],
                                    webb.ap()[:, gkey % 2, 7, :], start=False, stop=True).then_inc(s_fm.h, 1))
                            wait("vector", s_fm, fv)
                            s_fe.inc()
                            if e == 0:
                                em("vector", lambda v, nbl=nbl, e=e, nb=nb, slot=slot: v.tensor_scalar(
                                    out=ffn_q.ap()[:, nbl, :], in0=fz_ps[slot][0:128, 0:512],
                                    scalar1=ghat.ap()[:, nb, e:e + 1], scalar2=None,
                                    op0=ALU.mult).then_inc(s_fe.h, 1))
                            else:
                                em("vector", lambda v, nbl=nbl, e=e, nb=nb, slot=slot: v.scalar_tensor_tensor(
                                    out=ffn_q.ap()[:, nbl, :], in0=fz_ps[slot][0:128, 0:512],
                                    scalar=ghat.ap()[:, nb, e:e + 1], in1=ffn_q.ap()[:, nbl, :],
                                    op0=ALU.mult, op1=ALU.add).then_inc(s_fe.h, 1))
                            gstate[gkey]["done"] = s_fe.v
                            if e == NE - 1 and nbl == 7:
                                vv = s_vln.inc()
                                em("vector", lambda v: v.tensor_copy(out=ffn_q.ap()[:, 0, 0:1],
                                                                     in_=ffn_q.ap()[:, 0, 0:1]).then_inc(s_vln.h, 1))
                                wait("gpsimd", s_vln, vv)
                                fl = s_fl.inc(16)
                                em("gpsimd", lambda g, och=och, nbh=nbh: g.dma_start(
                                    out=ffn_d[och, nbh], in_=ffn_q.ap()).then_inc(s_fl.h, 16))
                                gstate["flush" + str(gkey // NE)] = fl
                        all_jobs.append({"run": job, "lc": group_lc if nbl == 0 else None})
        return all_jobs, gstate

    moe_all, moe_state = moe_jobs()
    moe_iter = iter(moe_all)
    for j in moe_all:
        fq_bulk.append(j)

    vst = {}
    qkst = {}
    cst = {}
    jobs0, vst[0] = v_jobs(0)
    for j in jobs0:
        fq_pri.append(j)
    jobs0, qkst[0] = qk_jobs(0)
    for j in jobs0:
        fq_pri.append(j)
    cjs, cj2, cst[0] = c_job(0, qkst[0])
    for cj in cjs:
        fq_pri.append({"run": cj, "lc": None})
    fq_pri.append({"run": cj2, "lc": None})
    while fq_pri:
        pop_one()

    pending = {"recips": None, "zbc": None}
    for hg in range(8):
        W = hg // 2
        wpar = W % 2
        hgl = hg % 2
        qpar = hg % 2
        wait("tensor", s_fe, qkst[hg]["done"])
        wait("tensor", s_fe, vst[W]["done"])
        wait("scalar", s_fe, cst[hg]["done"])
        for qb in range(QBS):
            if qb == 0 and hg + 1 < 8:
                newj = []
                if (hg + 1) % 2 == 0:
                    jw, vst[(hg + 1) // 2] = v_jobs((hg + 1) // 2)
                    newj += jw
                jq, qkst[hg + 1] = qk_jobs(hg + 1)
                newj += jq
                for j in newj:
                    if j.get("lc") is not None and not j.get("lc_done"):
                        j["lc"]()
                        j["lc_done"] = True
                    fq_pri.append(j)
                cjs, cj2, cst[hg + 1] = c_job(hg + 1, qkst[hg + 1])
                for cj in cjs:
                    fq_pri.append({"run": cj, "lc": None})
                fq_bulk.appendleft({"run": cj2, "lc": None})
            av_done_pair = {}
            for c in range(KC):
                if c in (1, 3, 5, 7) and pending["recips"]:
                    pending["recips"].pop(0)()
                    if not pending["recips"]:
                        pending["recips"] = None
                if c == 10 and pending["zbc"]:
                    pending["zbc"]()
                    pending["zbc"] = None
                for half in range(2):
                    if s_xp.v >= 2:
                        wait("tensor", s_xp, s_xp.v - 1 + half)
                    for j in range(2):
                        for i2 in range(2):
                            i = 2 * half + i2
                            last = (j == 1 and i2 == 1)
                            if last:
                                s_sc.inc()
                            em("tensor", lambda t, i=i, j=j, c=c, qb=qb, qpar=qpar, last=last: t.matmul(
                                sc_ps[64 * j:64 * j + 64, 512 * i:512 * i + 512],
                                qkT.ap()[32 * i:32 * i + 32, qpar, 1, c * 128 + 64 * j:c * 128 + 64 * j + 64],
                                qkT.ap()[32 * i:32 * i + 32, qpar, 0, qb * 512:(qb + 1) * 512],
                                start=True, stop=True,
                                tile_position=(32 * i, 64 * j)).then_inc(s_sc.h, 1) if last else t.matmul(
                                sc_ps[64 * j:64 * j + 64, 512 * i:512 * i + 512],
                                qkT.ap()[32 * i:32 * i + 32, qpar, 1, c * 128 + 64 * j:c * 128 + 64 * j + 64],
                                qkT.ap()[32 * i:32 * i + 32, qpar, 0, qb * 512:(qb + 1) * 512],
                                start=True, stop=True,
                                tile_position=(32 * i, 64 * j)))
                for half in range(2):
                    wait("scalar", s_sc, s_sc.v - 1 + half)
                    v_xp = s_xp.inc()
                    em("scalar", lambda s, c=c, half=half, hg=hg: s.activation(
                        out=pT.ap()[:, c % 2, half * 1024:(half + 1) * 1024],
                        in_=sc_ps[:, half * 1024:(half + 1) * 1024], func=AF.Exp,
                        scale=SCALE, bias=negc_bc.ap()[:, hg % 2:hg % 2 + 1]).then_inc(s_xp.h, 1))
                for half in range(2):
                    v_xp_h = s_xp.v - 1 + half
                    for cc in ([c - 1] if c >= 1 else []) + ([KC - 1] if c == KC - 1 else []):
                        if cc == KC - 1:
                            wait("tensor", s_xp, v_xp_h)
                        if cc == 0 and s_nmb.v:
                            wait("tensor", s_nmb, s_nmb.v)
                        for j in range(2):
                            hl = 2 * half + j
                            em("tensor", lambda t, hl=hl, cc=cc, hgl=hgl, wpar=wpar: t.matmul(
                                av_ps[32 * hl:32 * hl + 32, :],
                                vaug.ap()[:, wpar, cc, hgl * 4 + hl, :],
                                pT.ap()[:, cc % 2, hl * 512:(hl + 1) * 512],
                                start=(cc == 0), stop=(cc == KC - 1),
                                tile_position=(0, 32 * hl)))
                        for j in range(2):
                            hl = 2 * half + j
                            zc = (hl + 2) % 4
                            last = j == 1
                            if last:
                                s_avc.inc()
                                av_done_pair[(cc, half)] = s_avc.v
                            em("tensor", lambda t, hl=hl, zc=zc, cc=cc, last=last: t.matmul(
                                z_ps[32 * zc:32 * zc + 1, :],
                                ones_bf.ap(),
                                pT.ap()[:, cc % 2, hl * 512:(hl + 1) * 512],
                                start=(cc == 0), stop=(cc == KC - 1),
                                tile_position=(0, 32 * zc)).then_inc(s_avc.h, 1) if last else t.matmul(
                                z_ps[32 * zc:32 * zc + 1, :],
                                ones_bf.ap(),
                                pT.ap()[:, cc % 2, hl * 512:(hl + 1) * 512],
                                start=(cc == 0), stop=(cc == KC - 1),
                                tile_position=(0, 32 * zc)))
                if qb == 0 and c < 4:
                    pass
                elif qb == 0 and c >= 12:
                    pop_fillers(2)
                else:
                    pop_fillers(1)
            wait("vector", s_avc, s_avc.v)
            em("vector", lambda v: v.tensor_copy(out=av_m.ap()[:, 0, :], in_=av_ps[0:128, :]))
            s_nmb.inc()
            em("vector", lambda v: v.tensor_copy(out=av_m.ap()[:, 1, :],
                                                 in_=z_ps[0:128, :]).then_inc(s_nmb.h, 1))
            assert not pending["recips"] and pending["zbc"] is None

            def mk_recips():
                nmb_t = s_nmb.v
                fns = []
                for r in range(4):
                    def f(r=r):
                        if r == 0:
                            wait("vector", s_nmb, nmb_t)
                        if r < 3:
                            em("vector", lambda v, r=r: v.reciprocal(
                                out=zrec.ap()[32 * r:32 * r + 1, :], in_=av_m.ap()[32 * r:32 * r + 1, 1, :]))
                        else:
                            s_zr.inc()
                            em("vector", lambda v: v.reciprocal(
                                out=zrec.ap()[96:97, :], in_=av_m.ap()[96:97, 1, :]).then_inc(s_zr.h, 1))
                    fns.append(f)
                return fns

            def mk_zbc(hg=hg, qb=qb):
                v_zr_target = s_zr.v + 1

                def f():
                    slot = claim_slots(1)
                    wait("tensor", s_zr, v_zr_target)
                    for hl in range(4):
                        zc = (hl + 2) % 4
                        last = hl == 3
                        if last:
                            s_bc.inc()
                        em("tensor", lambda t, hl=hl, zc=zc, slot=slot, last=last: t.matmul(
                            fz_ps[slot][32 * hl:32 * hl + 32, :],
                            ones_col.ap()[32 * zc:32 * zc + 1, 0:32], zrec.ap()[32 * zc:32 * zc + 1, :],
                            start=True, stop=True,
                            tile_position=(32 * zc, 32 * hl)).then_inc(s_bc.h, 1) if last else t.matmul(
                            fz_ps[slot][32 * hl:32 * hl + 32, :],
                            ones_col.ap()[32 * zc:32 * zc + 1, 0:32], zrec.ap()[32 * zc:32 * zc + 1, :],
                            start=True, stop=True,
                            tile_position=(32 * zc, 32 * hl)))
                    wait("vector", s_bc, s_bc.v)
                    s_nm.inc()
                    s_fe.inc()
                    em("vector", lambda v, hg=hg, qb=qb, slot=slot: v.tensor_mul(
                        out=attT.ap()[:, hg, qb * 512:(qb + 1) * 512], in0=av_m.ap()[:, 0, :],
                        in1=fz_ps[slot][0:128, :]).then_inc(s_nm.h, 1).then_inc(s_fe.h, 1)
                        if False else v.tensor_mul(
                        out=attT.ap()[:, hg, qb * 512:(qb + 1) * 512], in0=av_m.ap()[:, 0, :],
                        in1=fz_ps[slot][0:128, :]).then_inc(s_nm.h, 1))
                    em("vector", lambda v, hg=hg, qb=qb: v.tensor_copy(
                        out=dmy.ap()[0:1, hg * 4 + qb:hg * 4 + qb + 1],
                        in_=ones_col.ap()[0:1, 0:1]).then_inc(s_fe.h, 1))
                return f
            pending["recips"] = mk_recips()
            pending["zbc"] = mk_zbc()
            pop_fillers(2)

    while pending["recips"]:
        pending["recips"].pop(0)()
    pending["recips"] = None
    if pending["zbc"]:
        pending["zbc"]()
        pending["zbc"] = None
    v_attn_nm = s_nm.v
    v_attn_mv = s_mv.v

    while fq_pri or fq_bulk:
        pop_one()
    wait("tensor", s_fe, fst["idx"])

    wait("sync", s_fe, fst["idx"])
    wait("sync", s_fl, s_fl.v)
    v_gbc = load(gbc.ap(), gam_d.broadcast_to([128, D]), s_gb)
    v_bbc = load(bbc.ap(), bet_d.broadcast_to([128, D]), s_gb)
    for oc in range(2):
        for g4 in range(2):
            staging_guard()
            v_wo = 0
            for gg in range(4):
                v_wo = load(staging.ap()[:, gg * 512:(gg + 1) * 512], wout_d[oc, g4 * 4 + gg])
            wait("gpsimd", s_ld, v_wo)
            if oc == 0 and g4 == 0:
                wait("gpsimd", s_fe, fst["idx"])
            c_wo = s_cast.inc()
            em("gpsimd", lambda g_, oc=oc, g4=g4: g_.tensor_copy(
                out=woutb.ap()[:, oc, g4 * 4:(g4 + 1) * 4, :],
                in_=staging.ap().rearrange("p (g c) -> p g c", g=4)).then_inc(s_cast.h, 1))
    wait("tensor", s_cast, s_cast.v)
    wait("tensor", s_nm, v_attn_nm)
    for nb in range(NB):
        for oc in range(2):
            bank = oc
            wait("tensor", s_me, s_me.v - 1)
            for g in range(8):
                em("tensor", lambda t, nb=nb, g=g, oc=oc, bank=bank: t.matmul(
                    mm_ps[bank][0:128, 0:512], attT.ap()[:, g, nb * 128:(nb + 1) * 128],
                    woutb.ap()[:, oc, g, :], start=(g == 0), stop=False))
            s_mm.inc()
            em("tensor", lambda t, oc=oc, bank=bank: t.matmul(
                mm_ps[bank][0:128, 0:512], ones_col.ap()[32:33, 0:128], rows.ap()[32:33, oc * 512:(oc + 1) * 512],
                start=False, stop=True, tile_position=(32, 0)).then_inc(s_mm.h, 1))
            wait("vector", s_mm, s_mm.v)
            s_me.inc()
            em("vector", lambda v, nb=nb, oc=oc, bank=bank: v.tensor_copy(
                out=projbf.ap()[:, nb, oc * 512:(oc + 1) * 512],
                in_=mm_ps[bank][0:128, 0:512]).then_inc(s_me.h, 1))
        em("vector", lambda v: v.drain())
        em("vector", lambda v, nb=nb: v.bn_stats(out=stats.ap()[:, 0:6], in_=projbf.ap()[:, nb, 0:512]))
        em("vector", lambda v, nb=nb: v.bn_stats(out=stats.ap()[:, 6:12], in_=projbf.ap()[:, nb, 512:1024]))
        em("vector", lambda v: v.drain())
        em("vector", lambda v: v.bn_aggr(out=mv.ap()[:, 0:2], in_=stats.ap()[:, 0:12]))
        em("vector", lambda v: v.drain())
        ln_chain(EPS1, ln1r.ap()[:, nb, 0:1], ln1r.ap()[:, nb, 1:2])

    wait("sync", s_mm, s_mm.v)
    wait("vector", s_gb, v_bbc)
    for nb in range(NB):
        par = nb % 2
        if nb >= 2:
            wait("sync", s_as, s_as.v - 2)
        v_x = load(xblk.ap()[:, par, :], x_d[nb * 128:(nb + 1) * 128, :], (s_x0, s_x1)[par])
        fch = (s_f0, s_f1)[par]
        for oc in range(2):
            v_f = load(ffn_rel.ap()[:, 2 * par + oc, :], ffn_d[oc, nb // 8, :, nb % 8, :], fch)
        wait("vector", (s_x0, s_x1)[par], v_x)
        wait("vector", fch, v_f)
        if nb >= 2:
            wait("vector", (s_st, s_s1)[par], ((s_st, s_s1)[par]).v)
        for step in range(5):
            if step:
                em("vector", lambda v: v.drain())
            for oc in range(2):
                if step == 0:
                    em("vector", lambda v, nb=nb, oc=oc: v.tensor_scalar(
                        out=asmA.ap()[:, oc, :], in0=projbf.ap()[:, nb, oc * 512:(oc + 1) * 512],
                        scalar1=ln1r.ap()[:, nb, 0:1], scalar2=ln1r.ap()[:, nb, 1:2],
                        op0=ALU.mult, op1=ALU.add))
                elif step == 1:
                    em("vector", lambda v, oc=oc: v.tensor_mul(
                        out=asmB.ap()[:, oc, :], in0=asmA.ap()[:, oc, :],
                        in1=gbc.ap()[:, oc * 512:(oc + 1) * 512]))
                elif step == 2:
                    em("vector", lambda v, oc=oc: v.tensor_add(
                        out=asmA.ap()[:, oc, :], in0=asmB.ap()[:, oc, :],
                        in1=bbc.ap()[:, oc * 512:(oc + 1) * 512]))
                elif step == 3:
                    em("vector", lambda v, oc=oc, par=par: v.tensor_add(
                        out=asmB.ap()[:, oc, :], in0=asmA.ap()[:, oc, :],
                        in1=xblk.ap()[:, par, oc * 512:(oc + 1) * 512]))
                else:
                    oslot = 2 * par + oc
                    s_as.inc()
                    em("vector", lambda v, oc=oc, par=par, oslot=oslot: v.tensor_add(
                        out=outb.ap()[:, oslot, :], in0=asmB.ap()[:, oc, :],
                        in1=ffn_rel.ap()[:, 2 * par + oc, :]).then_inc(s_as.h, 1))
        sch = (s_st, s_s1)[par]
        for oc in range(2):
            oslot = 2 * par + oc
            wait("scalar", s_as, s_as.v - (1 - oc))
            sch.inc(16)
            em("scalar", lambda s, nb=nb, oc=oc, oslot=oslot, sch=sch: s.dma_start(
                out=out_d[nb * 128:(nb + 1) * 128, oc * 512:(oc + 1) * 512],
                in_=outb.ap()[:, oslot, :]).then_inc(sch.h, 16))

    wait("scalar", s_st, s_st.v)
    wait("scalar", s_s1, s_s1.v)

    with nc.Block() as block:
        @block.sync
        def _(q):
            for f in ops["sync"]:
                f(q)

        @block.gpsimd
        def _(g):
            for f in ops["gpsimd"]:
                f(g)

        @block.tensor
        def _(t):
            for f in ops["tensor"]:
                f(t)

        @block.scalar
        def _(s):
            for f in ops["scalar"]:
                f(s)

        @block.vector
        def _(v):
            for f in ops["vector"]:
                f(v)

    es.close()
    return nc


def _prep_weights(w_qkv, w_out, b_out, ln_g, ln_b, Wg, bg, We, be):
    w_qkv = np.asarray(w_qkv, np.float32)
    w_out = np.asarray(w_out, np.float32)
    wqk = np.empty((8, 8, 128, 256), np.float32)
    for hg in range(8):
        cols = np.concatenate([w_qkv[:, hg * 128:(hg + 1) * 128],
                               w_qkv[:, 1024 + hg * 128:1024 + (hg + 1) * 128]], axis=1)
        wqk[hg] = cols.reshape(8, 128, 256)
    wv = np.ascontiguousarray(
        w_qkv[:, 2048:3072].reshape(8, 128, 4, 256).transpose(2, 0, 1, 3))
    wout = np.ascontiguousarray(w_out.reshape(8, 128, 2, 512).transpose(2, 0, 1, 3))
    wet = np.ascontiguousarray(
        np.asarray(We, np.float32).transpose(0, 2, 1).reshape(8, 8, 128, 2, 512).transpose(0, 3, 1, 2, 4))
    wg = np.ascontiguousarray(np.asarray(Wg, np.float32).reshape(8, 128, 8))
    blko = np.zeros((128, 4), BF)
    for h in range(4):
        blko[h * 32:(h + 1) * 32, h] = BF(1.0)
    return dict(
        wqk=np.ascontiguousarray(wqk), wv=wv, wout=wout, wet=wet, wg=wg,
        bout=np.ascontiguousarray(np.asarray(b_out, np.float32).reshape(1, 1024)),
        bg=np.ascontiguousarray(np.asarray(bg, np.float32).reshape(1, 8)),
        be=np.ascontiguousarray(np.asarray(be, np.float32)),
        gam=np.ascontiguousarray(np.asarray(ln_g, np.float32).reshape(1, 1024)),
        bet=np.ascontiguousarray(np.asarray(ln_b, np.float32).reshape(1, 1024)),
        ident_hbm=np.eye(128, dtype=BF),
        blko_hbm=blko,
    )


def kernel(L0_attn_x, L0_attn_w_qkv, L0_attn_w_out, L0_attn_b_out, L0_attn_ln_gamma,
           L0_attn_ln_beta, L0_ffn_Wg, L0_ffn_bg, L0_ffn_We, L0_ffn_be):
    x = np.ascontiguousarray(np.asarray(L0_attn_x, np.float32))
    w = _prep_weights(L0_attn_w_qkv, L0_attn_w_out, L0_attn_b_out, L0_attn_ln_gamma,
                      L0_attn_ln_beta, L0_ffn_Wg, L0_ffn_bg, L0_ffn_We, L0_ffn_be)
    nc = build_nc()
    in_maps = [dict(x=np.ascontiguousarray(x[b]), **w) for b in range(8)]
    res = run_bass_kernel_spmd(nc, in_maps, core_ids=list(range(8)))
    out = np.stack([r["out"] for r in res.results]).astype(np.float32)
    return out


if __name__ == "__main__":
    rng = np.random.default_rng(0)
    ins = dict(
        L0_attn_x=rng.standard_normal((8, 2048, 1024)).astype(np.float32),
        L0_attn_w_qkv=(rng.standard_normal((1024, 3072)) * 1024 ** -0.5).astype(np.float32),
        L0_attn_w_out=(rng.standard_normal((1024, 1024)) * 1024 ** -0.5).astype(np.float32),
        L0_attn_b_out=rng.standard_normal(1024).astype(np.float32) * 0.1,
        L0_attn_ln_gamma=(1 + 0.1 * rng.standard_normal(1024)).astype(np.float32),
        L0_attn_ln_beta=rng.standard_normal(1024).astype(np.float32) * 0.1,
        L0_ffn_Wg=(rng.uniform(-1, 1, (1024, 8)) / 32).astype(np.float32),
        L0_ffn_bg=rng.standard_normal(8).astype(np.float32) * 0.01,
        L0_ffn_We=(rng.uniform(-1, 1, (8, 1024, 1024)) / 32).astype(np.float32),
        L0_ffn_be=(rng.uniform(-1, 1, (8, 1024)) / 32).astype(np.float32),
    )
    out = kernel(**ins)
    print("out", out.shape, out.dtype, np.abs(out).mean())


# version 23
# speedup vs baseline: 1.0346x; 1.0346x over previous
import contextlib
import numpy as np
import ml_dtypes

import concourse.bass as bass
import concourse.mybir as mybir
from concourse.bass_utils import run_bass_kernel_spmd

BF = ml_dtypes.bfloat16
F32 = mybir.dt.float32
BF16 = mybir.dt.bfloat16
AF = mybir.ActivationFunctionType
ALU = mybir.AluOpType
AX = mybir.AxisListType

N = 2048
D = 1024
NB = 16
QBS = 4
KC = 16
NE = 8
SCALE = 32.0 ** -0.5
EPS0 = 1e-5
EPS1 = 1e-6



def build_nc():
    nc = bass.Bass("TRN2", target_bir_lowering=False, debug=False)

    x_d = nc.dram_tensor("x", [N, D], F32, kind="ExternalInput").ap()
    wqk_d = nc.dram_tensor("wqk", [8, 8, 128, 256], F32, kind="ExternalInput").ap()
    wv_d = nc.dram_tensor("wv", [4, 8, 128, 256], F32, kind="ExternalInput").ap()
    wout_d = nc.dram_tensor("wout", [2, 8, 128, 512], F32, kind="ExternalInput").ap()
    wet_d = nc.dram_tensor("wet", [NE, 2, 8, 128, 512], F32, kind="ExternalInput").ap()
    wg_d = nc.dram_tensor("wg", [8, 128, 8], F32, kind="ExternalInput").ap()
    bout_d = nc.dram_tensor("bout", [1, D], F32, kind="ExternalInput").ap()
    bg_d = nc.dram_tensor("bg", [1, 8], F32, kind="ExternalInput").ap()
    be_d = nc.dram_tensor("be", [8, D], F32, kind="ExternalInput").ap()
    gam_d = nc.dram_tensor("gam", [1, D], F32, kind="ExternalInput").ap()
    bet_d = nc.dram_tensor("bet", [1, D], F32, kind="ExternalInput").ap()
    ident_d = nc.dram_tensor("ident_hbm", [128, 128], BF16, kind="ExternalInput").ap()
    blko_d = nc.dram_tensor("blko_hbm", [128, 4], BF16, kind="ExternalInput").ap()
    out_d = nc.dram_tensor("out", [N, D], F32, kind="ExternalOutput").ap()
    ffn_d = nc.dram_tensor("ffn_scratch", [2, 2, 128, 8, 512], F32, kind="Internal").ap()

    A = lambda name, shape, dt: nc.alloc_sbuf_tensor(name, shape, dt)
    ident = A("ident", [128, 128], BF16)
    blko = A("blko", [128, 4], BF16)
    rows = A("rows", [128, D], F32)
    be_bf = A("be_bf", [8, D], BF16)
    ones_col = A("ones_col", [128, 128], F32)
    xn_tok = A("xn_tok", [128, 2, D], BF16)
    stats = A("stats", [128, 24], F32)
    mv = A("mv", [128, 4], F32)
    scr = A("scr", [128, 24], F32)
    ln1r = A("ln1r", [128, NB, 2], F32)
    gsm = A("gsm", [128, 33], F32)
    gp = A("gp", [128, 8], F32)
    gmask = A("gmask", [128, 8], F32)
    ghat = A("ghat", [128, NB, 8], F32)
    ghat_bf = A("ghat_bf", [128, 8], BF16)
    ghatT = A("ghatT", [8, N], BF16)
    nmax = A("nmax", [128, 13], F32)
    nmax_r = A("nmax_r", [1, 8], F32)
    dmy = A("dmy", [1, 64], F32)
    cs = A("cs", [1, 4], F32)
    negc_bc = A("negc_bc", [128, 2], F32)
    qkT = A("qkT", [128, 2, 2, N], BF16)
    ones_bf = A("ones_bf", [128, 1], BF16)
    wgb = A("wgb", [128, 8, 8], BF16)
    sq = xn_tok.ap().rearrange("p a b -> p (a b)")

    used_A = nc.bytes_allocated(ident.ap().space)
    assert used_A <= 55296, f"region A overflow: {used_A}"

    OFF_B = used_A
    xnT = nc.alloc_sbuf_tensor_at("xnT", [128, 8, N], BF16, offset=OFF_B)
    OFF_C = OFF_B + 32768
    attT = nc.alloc_sbuf_tensor_at("attT", [128, 8, N], BF16, offset=OFF_C)
    xblk = nc.alloc_sbuf_tensor_at("xblk", [128, 2, D], F32, offset=OFF_C)
    outb = nc.alloc_sbuf_tensor_at("outb", [128, 4, 512], F32, offset=OFF_C + 8192)
    asmA = nc.alloc_sbuf_tensor_at("asmA", [128, 2, 512], F32, offset=OFF_C + 16384)
    asmB = nc.alloc_sbuf_tensor_at("asmB", [128, 2, 512], F32, offset=OFF_C + 20480)
    OFF_D = OFF_C + 32768
    vaug = nc.alloc_sbuf_tensor_at("vaug", [128, 2, KC, 8, 32], BF16, offset=OFF_D)
    pT = nc.alloc_sbuf_tensor_at("pT", [128, 2, 2048], BF16, offset=OFF_D + 16384)
    zrec = nc.alloc_sbuf_tensor_at("zrec", [128, 512], F32, offset=OFF_D + 24576)
    av_m = nc.alloc_sbuf_tensor_at("av_m", [128, 2, 512], F32, offset=OFF_D + 26624)
    projbf = nc.alloc_sbuf_tensor_at("projbf", [128, NB, D], BF16, offset=OFF_D)
    OFF_E = OFF_D + 33792
    staging = nc.alloc_sbuf_tensor_at("staging", [128, 2048], F32, offset=OFF_E)
    wstream = nc.alloc_sbuf_tensor_at("wstream", [128, 2, 8, 256], BF16, offset=OFF_E + 8192)
    webb = nc.alloc_sbuf_tensor_at("webb", [128, 2, 8, 512], BF16, offset=OFF_E + 16384)
    ffn_q = nc.alloc_sbuf_tensor_at("ffn_q", [128, 8, 512], F32, offset=OFF_E + 32768)
    gbc = nc.alloc_sbuf_tensor_at("gbc", [128, D], F32, offset=OFF_E + 49152)
    bbc = nc.alloc_sbuf_tensor_at("bbc", [128, D], F32, offset=OFF_E + 53248)
    woutb = nc.alloc_sbuf_tensor_at("woutb", [128, 2, 8, 512], BF16, offset=OFF_E + 8192)
    ffn_rel = nc.alloc_sbuf_tensor_at("ffn_rel", [128, 4, 512], F32, offset=OFF_E + 24576)
    assert OFF_E + 57344 <= 207168, OFF_E

    arena = nc.alloc_psum_tensor("arena", [128, 4096], F32)
    sc_ps = arena.ap()[:, 0:2048]
    av_ps = arena.ap()[:, 2048:2560]
    z_ps = arena.ap()[:, 2560:3072]
    fz_ps = [arena.ap()[:, 3072:3584], arena.ap()[:, 3584:4096]]
    mm_ps = [arena.ap()[:, 0:512], arena.ap()[:, 512:1024]]
    tp_ps = [arena.ap()[:, 3072:3136].bitcast(BF16), arena.ap()[:, 3584:3648].bitcast(BF16)]

    es = contextlib.ExitStack()

    class Ctr:
        def __init__(self, name):
            self.h = es.enter_context(nc.semaphore(name))
            self.v = 0

        def inc(self, k=1):
            self.v += k
            return self.v

    s_ld = Ctr("s_ld")
    s_c = Ctr("s_c")
    s_x0 = Ctr("s_x0")
    s_x1 = Ctr("s_x1")
    s_gb = Ctr("s_gb")
    s_f0 = Ctr("s_f0")
    s_f1 = Ctr("s_f1")
    s_nx = Ctr("s_nx")
    s_fl = Ctr("s_fl")
    s_st = Ctr("s_st")
    s_s1 = Ctr("s_s1")
    s_mv = Ctr("s_mv")
    s_cast = Ctr("s_cast")
    s_vln = Ctr("s_vln")
    s_v2s = Ctr("s_v2s")
    s_s2v = Ctr("s_s2v")
    s_tp = Ctr("s_tp")
    s_tpe = Ctr("s_tpe")
    s_mm = Ctr("s_mm")
    s_me = Ctr("s_me")
    s_sc = Ctr("s_sc")
    s_xp = Ctr("s_xp")
    s_avc = Ctr("s_avc")
    s_zr = Ctr("s_zr")
    s_bc = Ctr("s_bc")
    s_nm = Ctr("s_nm")
    s_nmb = Ctr("s_nmb")
    s_fm = Ctr("s_fm")
    s_fe = Ctr("s_fe")
    s_as = Ctr("s_as")

    ops = {k: [] for k in ("sync", "tensor", "scalar", "vector", "gpsimd")}

    def em(eng, f):
        ops[eng].append(f)

    def load(dst_ap, src_ap, ch=None):
        ch = ch if ch is not None else s_ld
        v = ch.inc(16)
        em("sync", lambda q, d=dst_ap, s=src_ap, ch=ch: q.dma_start(out=d, in_=s).then_inc(ch.h, 16))
        return v

    def wait(eng, ctr, val):
        if val > 0:
            em(eng, lambda e, c=ctr, v=val: e.wait_ge(c.h, v))

    def staging_guard():
        wait("sync", s_cast, s_cast.v)

    import collections
    fq_pri = collections.deque()
    fq_bulk = collections.deque()
    fst = {"idx": 0, "popped": 0}

    def claim_slots(n):
        j = fst["idx"]
        fst["idx"] += n
        if n == 1:
            if j >= 2:
                wait("tensor", s_fe, j - 1)
            return j % 2
        else:
            if j >= 1:
                wait("tensor", s_fe, j)
            return None

    PF_LOOKAHEAD = 8

    def pop_one():
        q = fq_pri if fq_pri else fq_bulk
        if not q:
            return False
        job = q.popleft()
        fst["popped"] += 1
        seen = 0
        for qq in (fq_pri, fq_bulk):
            for jb in qq:
                if seen >= PF_LOOKAHEAD:
                    break
                lc = jb.get("lc")
                if lc is not None and not jb.get("lc_done"):
                    lc()
                    jb["lc_done"] = True
                seen += 1
            if seen >= PF_LOOKAHEAD:
                break
        if job.get("lc") is not None and not job.get("lc_done"):
            job["lc"]()
            job["lc_done"] = True
        job["run"]()
        return True

    def pop_fillers(n):
        for _ in range(n):
            if not pop_one():
                return

    load(ident.ap(), ident_d, s_c)
    load(blko.ap(), blko_d, s_c)
    load(rows.ap()[32:33, :], bout_d, s_c)
    load(rows.ap()[64:65, 0:8], bg_d, s_c)
    v_consts = load(rows.ap()[0:8, :], be_d, s_c)
    wait("tensor", s_c, v_consts)
    wait("vector", s_c, v_consts)

    em("vector", lambda v: v.memset(ones_col.ap(), 1.0))
    em("vector", lambda v: v.memset(ones_bf.ap(), 1.0))
    em("vector", lambda v: v.memset(z_ps, 0.0))
    em("vector", lambda v: v.drain())
    v_setup = s_vln.inc()
    em("vector", lambda v: v.tensor_copy(out=ones_col.ap()[:, 0:1], in_=ones_col.ap()[:, 0:1]).then_inc(s_vln.h, 1))

    wait("gpsimd", s_c, v_consts)
    c_bebf = s_cast.inc()
    em("gpsimd", lambda g: g.tensor_copy(out=be_bf.ap(), in_=rows.ap()[0:8, :]).then_inc(s_cast.h, 1))

    def ln_chain(eps, r_dst, negmur_dst, p=0):
        o = 8 * p
        v2s = s_v2s.inc()
        em("vector", lambda v, eps=eps, o=o, p=p: v.tensor_scalar(
            out=scr.ap()[:, o:o + 1], in0=mv.ap()[:, 2 * p + 1:2 * p + 2], scalar1=eps,
            scalar2=None, op0=ALU.add).then_inc(s_v2s.h, 1))
        wait("scalar", s_v2s, v2s)
        em("scalar", lambda s, o=o: s.activation(out=scr.ap()[:, o + 1:o + 2], in_=scr.ap()[:, o:o + 1], func=AF.Ln))
        em("scalar", lambda s: s.drain())
        s2v = s_s2v.inc()
        em("scalar", lambda s, o=o: s.activation(out=scr.ap()[:, o + 2:o + 3], in_=scr.ap()[:, o + 1:o + 2],
                                                 func=AF.Exp, scale=-0.5).then_inc(s_s2v.h, 1))
        wait("vector", s_s2v, s2v)
        if r_dst is not None:
            em("vector", lambda v, o=o: v.tensor_copy(out=r_dst, in_=scr.ap()[:, o + 2:o + 3]))
            em("vector", lambda v: v.drain())
        em("vector", lambda v, o=o, p=p: v.tensor_tensor(out=scr.ap()[:, o + 3:o + 4], in0=mv.ap()[:, 2 * p:2 * p + 1],
                                                         in1=scr.ap()[:, o + 2:o + 3], op=ALU.mult))
        em("vector", lambda v: v.drain())
        em("vector", lambda v, d=negmur_dst, o=o: v.tensor_scalar(
            out=d, in0=scr.ap()[:, o + 3:o + 4], scalar1=-1.0, scalar2=None, op0=ALU.mult))
        em("vector", lambda v: v.drain())

    staging_guard()
    v_wg = load(staging.ap()[:, 0:64], wg_d.rearrange("ic p c -> p ic c"))
    wait("gpsimd", s_ld, v_wg)
    c_wg = s_cast.inc()
    em("gpsimd", lambda g: g.tensor_copy(
        out=wgb.ap(), in_=staging.ap()[:, 0:64].rearrange("p (ic c) -> p ic c", ic=8)).then_inc(s_cast.h, 1))

    def gate_nb(nb):
        bank = nb % 2
        if nb == 0:
            wait("tensor", s_cast, c_wg)
        wait("tensor", s_me, s_me.v - 1)
        for ic in range(8):
            em("tensor", lambda t, nb=nb, ic=ic, bank=bank: t.matmul(
                mm_ps[bank][0:128, 0:8], xnT.ap()[:, ic, nb * 128:(nb + 1) * 128],
                wgb.ap()[:, ic, :], start=(ic == 0), stop=False))
        mmv = s_mm.inc()
        em("tensor", lambda t, bank=bank: t.matmul(
            mm_ps[bank][0:128, 0:8], ones_col.ap()[64:65, 0:128], rows.ap()[64:65, 0:8],
            start=False, stop=True, tile_position=(64, 0)).then_inc(s_mm.h, 1))
        wait("vector", s_mm, mmv)
        em("vector", lambda v, bank=bank: v.reduce_max(out=gsm.ap()[:, 0:1],
                                                       in_=mm_ps[bank][0:128, 0:8], axis=AX.X))
        em("vector", lambda v: v.drain())
        v2s = s_v2s.inc()
        em("vector", lambda v: v.tensor_scalar(out=gsm.ap()[:, 1:2], in0=gsm.ap()[:, 0:1], scalar1=-1.0,
                                               scalar2=None, op0=ALU.mult).then_inc(s_v2s.h, 1))
        wait("scalar", s_v2s, v2s)
        s2v = s_s2v.inc()
        em("scalar", lambda s, bank=bank: s.activation(out=gp.ap(), in_=mm_ps[bank][0:128, 0:8],
                                                       func=AF.Exp, bias=gsm.ap()[:, 1:2]).then_inc(s_s2v.h, 1))
        s_me.inc()
        em("scalar", lambda s, nb=nb: s.copy(out=gsm.ap()[:, 17 + nb:18 + nb],
                                             in_=ones_col.ap()[0:128, 0:1]).then_inc(s_me.h, 1))
        wait("vector", s_s2v, s2v)
        em("vector", lambda v: v.reduce_sum(out=gsm.ap()[:, 2:3], in_=gp.ap(), axis=AX.X))
        em("vector", lambda v: v.drain())
        em("vector", lambda v: v.reciprocal(out=gsm.ap()[:, 3:4], in_=gsm.ap()[:, 2:3]))
        em("vector", lambda v: v.drain())
        em("vector", lambda v: v.tensor_scalar(out=gp.ap(), in0=gp.ap(), scalar1=gsm.ap()[:, 3:4],
                                               scalar2=None, op0=ALU.mult))
        em("vector", lambda v: v.drain())
        em("vector", lambda v: v.reduce_max(out=gsm.ap()[:, 4:5], in_=gp.ap(), axis=AX.X))
        em("vector", lambda v: v.drain())
        em("vector", lambda v: v.tensor_scalar(out=gmask.ap(), in0=gp.ap(), scalar1=gsm.ap()[:, 4:5],
                                               scalar2=None, op0=ALU.is_equal))
        em("vector", lambda v: v.drain())
        em("vector", lambda v: v.scalar_tensor_tensor(out=gsm.ap()[:, 8:16], in0=gmask.ap(), scalar=-1e30,
                                                      in1=gp.ap(), op0=ALU.mult, op1=ALU.add))
        em("vector", lambda v: v.drain())
        em("vector", lambda v: v.reduce_max(out=gsm.ap()[:, 5:6], in_=gsm.ap()[:, 8:16], axis=AX.X))
        em("vector", lambda v: v.drain())
        em("vector", lambda v: v.tensor_scalar(out=gsm.ap()[:, 8:16], in0=gp.ap(), scalar1=gsm.ap()[:, 5:6],
                                               scalar2=None, op0=ALU.is_equal))
        em("vector", lambda v: v.drain())
        em("vector", lambda v: v.tensor_add(out=gmask.ap(), in0=gmask.ap(), in1=gsm.ap()[:, 8:16]))
        em("vector", lambda v: v.drain())
        em("vector", lambda v: v.tensor_mul(out=gmask.ap(), in0=gmask.ap(), in1=gp.ap()))
        em("vector", lambda v: v.tensor_add(out=gsm.ap()[:, 6:7], in0=gsm.ap()[:, 4:5], in1=gsm.ap()[:, 5:6]))
        em("vector", lambda v: v.drain())
        em("vector", lambda v: v.reciprocal(out=gsm.ap()[:, 7:8], in_=gsm.ap()[:, 6:7]))
        em("vector", lambda v: v.drain())
        em("vector", lambda v, nb=nb: v.tensor_scalar(out=ghat.ap()[:, nb, :], in0=gmask.ap(),
                                                      scalar1=gsm.ap()[:, 7:8], scalar2=None, op0=ALU.mult))
        em("vector", lambda v: v.drain())
        vln = s_vln.inc()
        em("vector", lambda v, nb=nb: v.tensor_copy(out=ghat_bf.ap(), in_=ghat.ap()[:, nb, :]).then_inc(s_vln.h, 1))
        wait("tensor", s_vln, vln)
        tpar = nb % 2
        wait("tensor", s_tpe, s_tpe.v - 1)
        tv = s_tp.inc()
        em("tensor", lambda t, tpar=tpar: t.transpose(
            out=tp_ps[tpar][0:8, 0:128], in_=ghat_bf.ap(), identity=ident.ap()).then_inc(s_tp.h, 1))
        wait("vector", s_tp, tv)
        s_tpe.inc()
        em("vector", lambda v, nb=nb, tpar=tpar: v.tensor_copy(
            out=ghatT.ap()[0:8, nb * 128:(nb + 1) * 128], in_=tp_ps[tpar][0:8, 0:128]).then_inc(s_tpe.h, 1))
    v_gate_done = None

    ln0_tp = {}
    ln0_xn = {}
    for nb0 in range(0, NB, 2):
        pair = (nb0, nb0 + 1)
        for nb in pair:
            par = nb % 2
            if nb >= 2:
                wait("sync", s_vln, ln0_xn[nb - 2])
            v_x = load(xblk.ap()[:, par, :], x_d[nb * 128:(nb + 1) * 128, :], (s_x0, s_x1)[par])
        wait("vector", s_x0, s_x0.v)
        wait("vector", s_x1, s_x1.v)
        for nb in pair:
            p = nb % 2
            em("vector", lambda v, p=p: v.bn_stats(out=stats.ap()[:, 12 * p:12 * p + 6],
                                                   in_=xblk.ap()[:, p, 0:512]))
            em("vector", lambda v, p=p: v.bn_stats(out=stats.ap()[:, 12 * p + 6:12 * p + 12],
                                                   in_=xblk.ap()[:, p, 512:1024]))
        em("vector", lambda v: v.drain())
        for nb in pair:
            p = nb % 2
            em("vector", lambda v, p=p: v.bn_aggr(out=mv.ap()[:, 2 * p:2 * p + 2],
                                                  in_=stats.ap()[:, 12 * p:12 * p + 12]))
        em("vector", lambda v: v.drain())
        for nb in pair:
            p = nb % 2
            ln_chain(EPS0, None, scr.ap()[:, 8 * p + 4:8 * p + 5], p)
        if nb0 >= 2:
            wait("vector", s_tp, ln0_tp[nb0 - 1])
        for nb in pair:
            p = nb % 2
            em("vector", lambda v, p=p: v.tensor_scalar(
                out=xn_tok.ap()[:, p, :], in0=xblk.ap()[:, p, :], scalar1=scr.ap()[:, 8 * p + 2:8 * p + 3],
                scalar2=scr.ap()[:, 8 * p + 4:8 * p + 5], op0=ALU.mult, op1=ALU.add).then_inc(s_vln.h, 1))
            vln = s_vln.inc()
            ln0_xn[nb] = vln
        wait("tensor", s_vln, s_vln.v)
        for fc in range(8):
            for nb in pair:
                p = nb % 2
                tpar = s_tp.v % 2
                if s_tp.v >= 2:
                    wait("tensor", s_tpe, s_tpe.v - 1)
                tv = s_tp.inc()
                em("tensor", lambda t, p=p, fc=fc, tpar=tpar: t.transpose(
                    out=tp_ps[tpar], in_=xn_tok.ap()[:, p, fc * 128:(fc + 1) * 128],
                    identity=ident.ap()).then_inc(s_tp.h, 1))
                wait("vector", s_tp, tv)
                s_tpe.inc()
                em("vector", lambda v, nb=nb, fc=fc, tpar=tpar: v.tensor_copy(
                    out=xnT.ap()[:, fc, nb * 128:(nb + 1) * 128], in_=tp_ps[tpar]).then_inc(s_tpe.h, 1))
        for nb in pair:
            ln0_tp[nb] = s_tp.v
        if nb0 >= 2:
            gate_nb(nb0 - 2)
            gate_nb(nb0 - 1)
        if nb0 == NB - 2:
            gate_nb(nb0)
            gate_nb(nb0 + 1)

    v_gate_done = s_tpe.v
    wait("tensor", s_tpe, v_gate_done)

    wchunk = {"seq": 0}
    wchunk_done = {}

    def make_stream_chunk(kind, key, src_chunks, dst_slot_ap_fn):
        seq = wchunk["seq"]
        wchunk["seq"] += 1
        par = seq % 2
        state = {}

        def emit_loadcast():
            staging_guard()
            v_w = 0
            for i, sc in enumerate(src_chunks):
                v_w = load(staging.ap()[:, i * 256:(i + 1) * 256], sc)
            wait("gpsimd", s_ld, v_w)
            if seq >= 2:
                assert (seq - 2) in wchunk_done, f"stream chunk {seq}: prefetch too deep"
                wait("gpsimd", s_fe, wchunk_done[seq - 2])
            cv = s_cast.inc()
            em("gpsimd", lambda g, par=par: g.tensor_copy(
                out=dst_slot_ap_fn(par),
                in_=staging.ap().rearrange("p (i c) -> p i c", i=8)).then_inc(s_cast.h, 1))
            state["cast"] = cv
        return seq, par, emit_loadcast, state

    def qk_jobs(hg):
        qpar = hg % 2
        seq, wpar, emit_loadcast, lcstate = make_stream_chunk(
            "qk", hg, [wqk_d[hg, fc] for fc in range(8)],
            lambda par: wstream.ap()[:, par, :, :])
        state = {}
        jobs = []
        for grp in range(8):
            cb, nbq = grp // 4, grp % 4

            def job(grp=grp, cb=cb, nbq=nbq, qpar=qpar, wpar=wpar, seq=seq):
                slot = claim_slots(1)
                wait("tensor", s_cast, lcstate["cast"])
                fv = s_fm.inc()
                for fc in range(8):
                    last = fc == 7
                    em("tensor", lambda t, fc=fc, cb=cb, nbq=nbq, slot=slot, qpar=qpar, wpar=wpar, last=last: t.matmul(
                        fz_ps[slot][0:128, 0:512],
                        wstream.ap()[:, wpar, fc, cb * 128:(cb + 1) * 128],
                        xnT.ap()[:, fc, nbq * 512:(nbq + 1) * 512],
                        start=(fc == 0), stop=last).then_inc(s_fm.h, 1) if last else t.matmul(
                        fz_ps[slot][0:128, 0:512],
                        wstream.ap()[:, wpar, fc, cb * 128:(cb + 1) * 128],
                        xnT.ap()[:, fc, nbq * 512:(nbq + 1) * 512],
                        start=(fc == 0), stop=last))
                wait("vector", s_fm, fv)
                s_fe.inc()
                em("vector", lambda v, cb=cb, nbq=nbq, slot=slot, qpar=qpar: v.tensor_copy(
                    out=qkT.ap()[:, qpar, cb, nbq * 512:(nbq + 1) * 512],
                    in_=fz_ps[slot][0:128, 0:512]).then_inc(s_fe.h, 1))
                if grp == 7:
                    state["done"] = s_fe.v
                    wchunk_done[seq] = s_fe.v
            jobs.append({"run": job, "lc": emit_loadcast if grp == 0 else None})
        return jobs, state

    def v_jobs(W):
        wpar_v = W % 2
        seq, wpar, emit_loadcast, lcstate = make_stream_chunk(
            "wv", W, [wv_d[W, ic] for ic in range(8)],
            lambda par: wstream.ap()[:, par, :, :])
        state = {}
        jobs = []
        for nb in range(NB):
            def job(nb=nb, wpar_v=wpar_v, wpar=wpar, seq=seq):
                slot = claim_slots(1)
                wait("tensor", s_cast, lcstate["cast"])
                fv = s_fm.inc()
                for ic in range(8):
                    last = ic == 7
                    em("tensor", lambda t, ic=ic, nb=nb, slot=slot, wpar=wpar, last=last: t.matmul(
                        fz_ps[slot][0:128, 0:256],
                        xnT.ap()[:, ic, nb * 128:(nb + 1) * 128],
                        wstream.ap()[:, wpar, ic, :],
                        start=(ic == 0), stop=last).then_inc(s_fm.h, 1) if last else t.matmul(
                        fz_ps[slot][0:128, 0:256],
                        xnT.ap()[:, ic, nb * 128:(nb + 1) * 128],
                        wstream.ap()[:, wpar, ic, :],
                        start=(ic == 0), stop=last))
                wait("vector", s_fm, fv)
                s_fe.inc()
                em("vector", lambda v, nb=nb, slot=slot, wpar_v=wpar_v: v.tensor_copy(
                    out=vaug.ap()[:, wpar_v, nb, :, :],
                    in_=fz_ps[slot][0:128, 0:256].rearrange("p (h d) -> p h d", h=8)).then_inc(s_fe.h, 1))
                if nb == NB - 1:
                    state["done"] = s_fe.v
                    wchunk_done[seq] = s_fe.v
            jobs.append({"run": job, "lc": emit_loadcast if nb == 0 else None})
        return jobs, state

    def c_job(hg, qk_state):
        state = {}
        sv = {}

        def sq_job(side):
            def f(side=side):
                qpar = hg % 2
                if side == 0:
                    wait("vector", s_fe, qk_state["done"])
                else:
                    wait("vector", s_mm, sv["mm0_last"])
                em("vector", lambda v, side=side, qpar=qpar: v.tensor_mul(
                    out=sq, in0=qkT.ap()[:, qpar, side, :], in1=qkT.ap()[:, qpar, side, :]))
                em("vector", lambda v: v.drain())
                vln = s_vln.inc()
                em("vector", lambda v: v.tensor_copy(out=sq[:, 0:1], in_=sq[:, 0:1]).then_inc(s_vln.h, 1))
                sv["sq" + str(side)] = vln
            return f

        def mm_job(side):
            def f(side=side):
                claim_slots(2)
                wait("tensor", s_vln, sv["sq" + str(side)])
                for nbq in range(4):
                    pslot = nbq % 2
                    mmv = s_mm.inc()
                    em("tensor", lambda t, nbq=nbq, pslot=pslot: t.matmul(
                        fz_ps[pslot][0:4, 0:512], blko.ap(), sq[:, nbq * 512:(nbq + 1) * 512],
                        start=True, stop=True).then_inc(s_mm.h, 1))
                    if nbq == 3 and side == 0:
                        sv["mm0_last"] = s_mm.v
                    wait("vector", s_mm, mmv)
                    mev = s_me.inc()
                    em("vector", lambda v, side=side, nbq=nbq, pslot=pslot: v.reduce_max(
                        out=nmax.ap()[0:4, side * 4 + nbq:side * 4 + nbq + 1],
                        in_=fz_ps[pslot][0:4, 0:512], axis=AX.X).then_inc(s_me.h, 1))
                    if nbq >= 1:
                        wait("tensor", s_me, mev - 1)
                wait("tensor", s_me, s_me.v)
                s_fe.inc(2)
                em("vector", lambda v, side=side: v.tensor_copy(
                    out=nmax.ap()[0:4, 11 + side:12 + side],
                    in_=ones_col.ap()[0:4, 0:1]).then_inc(s_fe.h, 2))
                if side == 1:
                    finish()
            return f

        def finish():
            em("vector", lambda v: v.drain())
            em("vector", lambda v: v.reduce_max(out=nmax.ap()[0:4, 8:9], in_=nmax.ap()[0:4, 0:4], axis=AX.X))
            em("vector", lambda v: v.reduce_max(out=nmax.ap()[0:4, 9:10], in_=nmax.ap()[0:4, 4:8], axis=AX.X))
            em("vector", lambda v: v.drain())
            vln = s_vln.inc()
            em("vector", lambda v: v.tensor_copy(out=nmax.ap()[0:4, 10:11],
                                                 in_=nmax.ap()[0:4, 8:9]).then_inc(s_vln.h, 1))
            wait("gpsimd", s_vln, vln)
            mvv = s_nx.inc(16)
            em("gpsimd", lambda g: g.dma_start(out=nmax_r.ap()[0:1, 0:8],
                                               in_=nmax.ap()[0:4, 8:10]).then_inc(s_nx.h, 16))
            wait("vector", s_nx, mvv)
            em("vector", lambda v: v.reduce_max(out=cs.ap()[0:1, 0:1], in_=nmax_r.ap()[0:1, 0:8:2], axis=AX.X))
            em("vector", lambda v: v.reduce_max(out=cs.ap()[0:1, 1:2], in_=nmax_r.ap()[0:1, 1:8:2], axis=AX.X))
            em("vector", lambda v: v.drain())
            em("vector", lambda v: v.tensor_tensor(out=cs.ap()[0:1, 2:3], in0=cs.ap()[0:1, 0:1],
                                                   in1=cs.ap()[0:1, 1:2], op=ALU.add))
            em("vector", lambda v: v.drain())
            vln = s_vln.inc()
            em("vector", lambda v: v.tensor_scalar(out=cs.ap()[0:1, 0:1], in0=cs.ap()[0:1, 2:3],
                                                   scalar1=-SCALE * 0.5, scalar2=None,
                                                   op0=ALU.mult).then_inc(s_vln.h, 1))
            state["negc_ready"] = s_vln.v

        def negc_job():
            slot = claim_slots(1)
            wait("tensor", s_vln, state["negc_ready"])
            mmv = s_mm.inc()
            em("tensor", lambda t, slot=slot: t.matmul(
                fz_ps[slot][0:128, 0:1], ones_col.ap()[0:1, 0:128], cs.ap()[0:1, 0:1],
                start=True, stop=True).then_inc(s_mm.h, 1))
            wait("vector", s_mm, mmv)
            s_fe.inc()
            em("vector", lambda v, hg=hg, slot=slot: v.tensor_copy(
                out=negc_bc.ap()[:, hg % 2:hg % 2 + 1], in_=fz_ps[slot][0:128, 0:1]).then_inc(s_fe.h, 1))
            state["done"] = s_fe.v
        return [sq_job(0), mm_job(0), sq_job(1), mm_job(1)], negc_job, state

    def moe_jobs():
        all_jobs = []
        gstate = {}
        for och in range(2):
            for nbh in range(2):
                for e in range(NE):
                    gkey = (och * 2 + nbh) * NE + e

                    def group_lc(och=och, e=e, gkey=gkey):
                        cvs = []
                        for rnd in range(2):
                            staging_guard()
                            v_w = 0
                            for ic4 in range(4):
                                ic = rnd * 4 + ic4
                                v_w = load(staging.ap()[:, ic4 * 512:(ic4 + 1) * 512], wet_d[e, och, ic])
                            wait("gpsimd", s_ld, v_w)
                            if rnd == 0 and gkey >= 2:
                                assert "done" in gstate[gkey - 2], f"moe group {gkey}: prefetch too deep"
                                wait("gpsimd", s_fe, gstate[gkey - 2]["done"])
                            if gkey == 0 and rnd == 0:
                                wait("gpsimd", s_cast, c_bebf)
                            cv = s_cast.inc()
                            em("gpsimd", lambda g, gkey=gkey, rnd=rnd: g.tensor_copy(
                                out=webb.ap()[:, gkey % 2, rnd * 4:(rnd + 1) * 4, :],
                                in_=staging.ap().rearrange("p (i c) -> p i c", i=4)).then_inc(s_cast.h, 1))
                            cvs.append(cv)
                        gstate[gkey]["cast0"] = cvs[0]
                        gstate[gkey]["cast"] = cvs[1]

                    gstate[gkey] = {}
                    for nbl in range(8):
                        nb = nbh * 8 + nbl

                        def job(och=och, nbh=nbh, e=e, nb=nb, nbl=nbl, gkey=gkey):
                            slot = claim_slots(1)
                            wait("tensor", s_cast, gstate[gkey]["cast0"])
                            if e == 0 and nbl == 0 and gkey >= NE:
                                wait("vector", s_fl, gstate["flush" + str(gkey // NE - 1)])
                            fv = s_fm.inc()
                            n_ic = 8 if e == 0 else 7
                            for ic in range(n_ic):
                                if ic == 4:
                                    wait("tensor", s_cast, gstate[gkey]["cast"])
                                em("tensor", lambda t, nb=nb, ic=ic, gkey=gkey, slot=slot: t.matmul(
                                    fz_ps[slot][0:128, 0:512], xnT.ap()[:, ic, nb * 128:(nb + 1) * 128],
                                    webb.ap()[:, gkey % 2, ic, :], start=(ic == 0), stop=False))
                            if e == 0:
                                em("tensor", lambda t, nb=nb, och=och, slot=slot: t.matmul(
                                    fz_ps[slot][0:128, 0:512], ghatT.ap()[0:8, nb * 128:(nb + 1) * 128],
                                    be_bf.ap()[0:8, och * 512:(och + 1) * 512],
                                    start=False, stop=True).then_inc(s_fm.h, 1))
                            else:
                                em("tensor", lambda t, nb=nb, gkey=gkey, slot=slot: t.matmul(
                                    fz_ps[slot][0:128, 0:512], xnT.ap()[:, 7, nb * 128:(nb + 1) * 128],
                                    webb.ap()[:, gkey % 2, 7, :], start=False, stop=True).then_inc(s_fm.h, 1))
                            wait("vector", s_fm, fv)
                            s_fe.inc()
                            if e == 0:
                                em("vector", lambda v, nbl=nbl, e=e, nb=nb, slot=slot: v.tensor_scalar(
                                    out=ffn_q.ap()[:, nbl, :], in0=fz_ps[slot][0:128, 0:512],
                                    scalar1=ghat.ap()[:, nb, e:e + 1], scalar2=None,
                                    op0=ALU.mult).then_inc(s_fe.h, 1))
                            else:
                                em("vector", lambda v, nbl=nbl, e=e, nb=nb, slot=slot: v.scalar_tensor_tensor(
                                    out=ffn_q.ap()[:, nbl, :], in0=fz_ps[slot][0:128, 0:512],
                                    scalar=ghat.ap()[:, nb, e:e + 1], in1=ffn_q.ap()[:, nbl, :],
                                    op0=ALU.mult, op1=ALU.add).then_inc(s_fe.h, 1))
                            gstate[gkey]["done"] = s_fe.v
                            if e == NE - 1 and nbl == 7:
                                vv = s_vln.inc()
                                em("vector", lambda v: v.tensor_copy(out=ffn_q.ap()[:, 0, 0:1],
                                                                     in_=ffn_q.ap()[:, 0, 0:1]).then_inc(s_vln.h, 1))
                                wait("gpsimd", s_vln, vv)
                                fl = s_fl.inc(16)
                                em("gpsimd", lambda g, och=och, nbh=nbh: g.dma_start(
                                    out=ffn_d[och, nbh], in_=ffn_q.ap()).then_inc(s_fl.h, 16))
                                gstate["flush" + str(gkey // NE)] = fl
                        all_jobs.append({"run": job, "lc": group_lc if nbl == 0 else None})
        return all_jobs, gstate

    moe_all, moe_state = moe_jobs()
    moe_iter = iter(moe_all)
    for j in moe_all:
        fq_bulk.append(j)

    vst = {}
    qkst = {}
    cst = {}
    jobs0, vst[0] = v_jobs(0)
    for j in jobs0:
        fq_pri.append(j)
    jobs0, qkst[0] = qk_jobs(0)
    for j in jobs0:
        fq_pri.append(j)
    cjs, cj2, cst[0] = c_job(0, qkst[0])
    for cj in cjs:
        fq_pri.append({"run": cj, "lc": None})
    fq_pri.append({"run": cj2, "lc": None})
    while fq_pri:
        pop_one()

    pending = {"recips": None, "zbc": None}
    for hg in range(8):
        W = hg // 2
        wpar = W % 2
        hgl = hg % 2
        qpar = hg % 2
        wait("tensor", s_fe, qkst[hg]["done"])
        wait("tensor", s_fe, vst[W]["done"])
        wait("scalar", s_fe, cst[hg]["done"])
        for qb in range(QBS):
            if qb == 0 and hg + 1 < 8:
                newj = []
                if (hg + 1) % 2 == 0:
                    jw, vst[(hg + 1) // 2] = v_jobs((hg + 1) // 2)
                    newj += jw
                jq, qkst[hg + 1] = qk_jobs(hg + 1)
                newj += jq
                for j in newj:
                    if j.get("lc") is not None and not j.get("lc_done"):
                        j["lc"]()
                        j["lc_done"] = True
                    fq_pri.append(j)
                cjs, cj2, cst[hg + 1] = c_job(hg + 1, qkst[hg + 1])
                for cj in cjs:
                    fq_pri.append({"run": cj, "lc": None})
                fq_bulk.appendleft({"run": cj2, "lc": None})
            av_done_pair = {}
            for c in range(KC):
                if c in (1, 3, 5, 7) and pending["recips"]:
                    pending["recips"].pop(0)()
                    if not pending["recips"]:
                        pending["recips"] = None
                if c == 10 and pending["zbc"]:
                    pending["zbc"]()
                    pending["zbc"] = None
                for half in range(2):
                    if s_xp.v >= 2:
                        wait("tensor", s_xp, s_xp.v - 1 + half)
                    for j in range(2):
                        for i2 in range(2):
                            i = 2 * half + i2
                            last = (j == 1 and i2 == 1)
                            if last:
                                s_sc.inc()
                            em("tensor", lambda t, i=i, j=j, c=c, qb=qb, qpar=qpar, last=last: t.matmul(
                                sc_ps[64 * j:64 * j + 64, 512 * i:512 * i + 512],
                                qkT.ap()[32 * i:32 * i + 32, qpar, 1, c * 128 + 64 * j:c * 128 + 64 * j + 64],
                                qkT.ap()[32 * i:32 * i + 32, qpar, 0, qb * 512:(qb + 1) * 512],
                                start=True, stop=True,
                                tile_position=(32 * i, 64 * j)).then_inc(s_sc.h, 1) if last else t.matmul(
                                sc_ps[64 * j:64 * j + 64, 512 * i:512 * i + 512],
                                qkT.ap()[32 * i:32 * i + 32, qpar, 1, c * 128 + 64 * j:c * 128 + 64 * j + 64],
                                qkT.ap()[32 * i:32 * i + 32, qpar, 0, qb * 512:(qb + 1) * 512],
                                start=True, stop=True,
                                tile_position=(32 * i, 64 * j)))
                for half in range(2):
                    wait("scalar", s_sc, s_sc.v - 1 + half)
                    v_xp = s_xp.inc()
                    em("scalar", lambda s, c=c, half=half, hg=hg: s.activation(
                        out=pT.ap()[:, c % 2, half * 1024:(half + 1) * 1024],
                        in_=sc_ps[:, half * 1024:(half + 1) * 1024], func=AF.Exp,
                        scale=SCALE, bias=negc_bc.ap()[:, hg % 2:hg % 2 + 1]).then_inc(s_xp.h, 1))
                for half in range(2):
                    v_xp_h = s_xp.v - 1 + half
                    for cc in ([c - 1] if c >= 1 else []) + ([KC - 1] if c == KC - 1 else []):
                        if cc == KC - 1:
                            wait("tensor", s_xp, v_xp_h)
                        if cc == 0 and s_nmb.v:
                            wait("tensor", s_nmb, s_nmb.v)
                        for j in range(2):
                            hl = 2 * half + j
                            em("tensor", lambda t, hl=hl, cc=cc, hgl=hgl, wpar=wpar: t.matmul(
                                av_ps[32 * hl:32 * hl + 32, :],
                                vaug.ap()[:, wpar, cc, hgl * 4 + hl, :],
                                pT.ap()[:, cc % 2, hl * 512:(hl + 1) * 512],
                                start=(cc == 0), stop=(cc == KC - 1),
                                tile_position=(0, 32 * hl)))
                        for j in range(2):
                            hl = 2 * half + j
                            zc = (hl + 2) % 4
                            last = j == 1
                            if last:
                                s_avc.inc()
                                av_done_pair[(cc, half)] = s_avc.v
                            em("tensor", lambda t, hl=hl, zc=zc, cc=cc, last=last: t.matmul(
                                z_ps[32 * zc:32 * zc + 1, :],
                                ones_bf.ap(),
                                pT.ap()[:, cc % 2, hl * 512:(hl + 1) * 512],
                                start=(cc == 0), stop=(cc == KC - 1),
                                tile_position=(0, 32 * zc)).then_inc(s_avc.h, 1) if last else t.matmul(
                                z_ps[32 * zc:32 * zc + 1, :],
                                ones_bf.ap(),
                                pT.ap()[:, cc % 2, hl * 512:(hl + 1) * 512],
                                start=(cc == 0), stop=(cc == KC - 1),
                                tile_position=(0, 32 * zc)))
                if qb == 0 and c < 4:
                    pass
                elif qb == 0 and c >= 12:
                    pop_fillers(2)
                else:
                    pop_fillers(1)
            wait("vector", s_avc, s_avc.v)
            em("vector", lambda v: v.tensor_copy(out=av_m.ap()[:, 0, :], in_=av_ps[0:128, :]))
            s_nmb.inc()
            em("vector", lambda v: v.tensor_copy(out=av_m.ap()[:, 1, :],
                                                 in_=z_ps[0:128, :]).then_inc(s_nmb.h, 1))
            assert not pending["recips"] and pending["zbc"] is None

            def mk_recips():
                nmb_t = s_nmb.v
                fns = []
                for r in range(4):
                    def f(r=r):
                        if r == 0:
                            wait("vector", s_nmb, nmb_t)
                        if r < 3:
                            em("vector", lambda v, r=r: v.reciprocal(
                                out=zrec.ap()[32 * r:32 * r + 1, :], in_=av_m.ap()[32 * r:32 * r + 1, 1, :]))
                        else:
                            s_zr.inc()
                            em("vector", lambda v: v.reciprocal(
                                out=zrec.ap()[96:97, :], in_=av_m.ap()[96:97, 1, :]).then_inc(s_zr.h, 1))
                    fns.append(f)
                return fns

            def mk_zbc(hg=hg, qb=qb):
                v_zr_target = s_zr.v + 1

                def f():
                    slot = claim_slots(1)
                    wait("tensor", s_zr, v_zr_target)
                    for hl in range(4):
                        zc = (hl + 2) % 4
                        last = hl == 3
                        if last:
                            s_bc.inc()
                        em("tensor", lambda t, hl=hl, zc=zc, slot=slot, last=last: t.matmul(
                            fz_ps[slot][32 * hl:32 * hl + 32, :],
                            ones_col.ap()[32 * zc:32 * zc + 1, 0:32], zrec.ap()[32 * zc:32 * zc + 1, :],
                            start=True, stop=True,
                            tile_position=(32 * zc, 32 * hl)).then_inc(s_bc.h, 1) if last else t.matmul(
                            fz_ps[slot][32 * hl:32 * hl + 32, :],
                            ones_col.ap()[32 * zc:32 * zc + 1, 0:32], zrec.ap()[32 * zc:32 * zc + 1, :],
                            start=True, stop=True,
                            tile_position=(32 * zc, 32 * hl)))
                    wait("vector", s_bc, s_bc.v)
                    s_nm.inc()
                    s_fe.inc()
                    em("vector", lambda v, hg=hg, qb=qb, slot=slot: v.tensor_mul(
                        out=attT.ap()[:, hg, qb * 512:(qb + 1) * 512], in0=av_m.ap()[:, 0, :],
                        in1=fz_ps[slot][0:128, :]).then_inc(s_nm.h, 1).then_inc(s_fe.h, 1)
                        if False else v.tensor_mul(
                        out=attT.ap()[:, hg, qb * 512:(qb + 1) * 512], in0=av_m.ap()[:, 0, :],
                        in1=fz_ps[slot][0:128, :]).then_inc(s_nm.h, 1))
                    em("vector", lambda v, hg=hg, qb=qb: v.tensor_copy(
                        out=dmy.ap()[0:1, hg * 4 + qb:hg * 4 + qb + 1],
                        in_=ones_col.ap()[0:1, 0:1]).then_inc(s_fe.h, 1))
                return f
            pending["recips"] = mk_recips()
            pending["zbc"] = mk_zbc()
            pop_fillers(2)

    while pending["recips"]:
        pending["recips"].pop(0)()
    pending["recips"] = None
    if pending["zbc"]:
        pending["zbc"]()
        pending["zbc"] = None
    v_attn_nm = s_nm.v
    v_attn_mv = s_mv.v

    while fq_pri or fq_bulk:
        pop_one()
    wait("tensor", s_fe, fst["idx"])

    wait("sync", s_fe, fst["idx"])
    wait("sync", s_fl, s_fl.v)
    v_gbc = load(gbc.ap(), gam_d.broadcast_to([128, D]), s_gb)
    v_bbc = load(bbc.ap(), bet_d.broadcast_to([128, D]), s_gb)
    for oc in range(2):
        for g4 in range(2):
            staging_guard()
            v_wo = 0
            for gg in range(4):
                v_wo = load(staging.ap()[:, gg * 512:(gg + 1) * 512], wout_d[oc, g4 * 4 + gg])
            wait("gpsimd", s_ld, v_wo)
            if oc == 0 and g4 == 0:
                wait("gpsimd", s_fe, fst["idx"])
            c_wo = s_cast.inc()
            em("gpsimd", lambda g_, oc=oc, g4=g4: g_.tensor_copy(
                out=woutb.ap()[:, oc, g4 * 4:(g4 + 1) * 4, :],
                in_=staging.ap().rearrange("p (g c) -> p g c", g=4)).then_inc(s_cast.h, 1))
    wait("tensor", s_cast, s_cast.v)
    wait("tensor", s_nm, v_attn_nm)
    for nb in range(NB):
        for oc in range(2):
            bank = oc
            wait("tensor", s_me, s_me.v - 1)
            for g in range(8):
                em("tensor", lambda t, nb=nb, g=g, oc=oc, bank=bank: t.matmul(
                    mm_ps[bank][0:128, 0:512], attT.ap()[:, g, nb * 128:(nb + 1) * 128],
                    woutb.ap()[:, oc, g, :], start=(g == 0), stop=False))
            s_mm.inc()
            em("tensor", lambda t, oc=oc, bank=bank: t.matmul(
                mm_ps[bank][0:128, 0:512], ones_col.ap()[32:33, 0:128], rows.ap()[32:33, oc * 512:(oc + 1) * 512],
                start=False, stop=True, tile_position=(32, 0)).then_inc(s_mm.h, 1))
            wait("vector", s_mm, s_mm.v)
            s_me.inc()
            em("vector", lambda v, nb=nb, oc=oc, bank=bank: v.tensor_copy(
                out=projbf.ap()[:, nb, oc * 512:(oc + 1) * 512],
                in_=mm_ps[bank][0:128, 0:512]).then_inc(s_me.h, 1))
        em("vector", lambda v: v.drain())
        em("vector", lambda v, nb=nb: v.bn_stats(out=stats.ap()[:, 0:6], in_=projbf.ap()[:, nb, 0:512]))
        em("vector", lambda v, nb=nb: v.bn_stats(out=stats.ap()[:, 6:12], in_=projbf.ap()[:, nb, 512:1024]))
        em("vector", lambda v: v.drain())
        em("vector", lambda v: v.bn_aggr(out=mv.ap()[:, 0:2], in_=stats.ap()[:, 0:12]))
        em("vector", lambda v: v.drain())
        ln_chain(EPS1, ln1r.ap()[:, nb, 0:1], ln1r.ap()[:, nb, 1:2])

    wait("sync", s_mm, s_mm.v)
    wait("vector", s_gb, v_bbc)
    for nb in range(NB):
        par = nb % 2
        if nb >= 2:
            wait("sync", s_as, s_as.v - 2)
        v_x = load(xblk.ap()[:, par, :], x_d[nb * 128:(nb + 1) * 128, :], (s_x0, s_x1)[par])
        fch = (s_f0, s_f1)[par]
        for oc in range(2):
            v_f = load(ffn_rel.ap()[:, 2 * par + oc, :], ffn_d[oc, nb // 8, :, nb % 8, :], fch)
        wait("vector", (s_x0, s_x1)[par], v_x)
        wait("vector", fch, v_f)
        if nb >= 2:
            wait("vector", (s_st, s_s1)[par], ((s_st, s_s1)[par]).v)
        for step in range(5):
            if step:
                em("vector", lambda v: v.drain())
            for oc in range(2):
                if step == 0:
                    em("vector", lambda v, nb=nb, oc=oc: v.tensor_scalar(
                        out=asmA.ap()[:, oc, :], in0=projbf.ap()[:, nb, oc * 512:(oc + 1) * 512],
                        scalar1=ln1r.ap()[:, nb, 0:1], scalar2=ln1r.ap()[:, nb, 1:2],
                        op0=ALU.mult, op1=ALU.add))
                elif step == 1:
                    em("vector", lambda v, oc=oc: v.tensor_mul(
                        out=asmB.ap()[:, oc, :], in0=asmA.ap()[:, oc, :],
                        in1=gbc.ap()[:, oc * 512:(oc + 1) * 512]))
                elif step == 2:
                    em("vector", lambda v, oc=oc: v.tensor_add(
                        out=asmA.ap()[:, oc, :], in0=asmB.ap()[:, oc, :],
                        in1=bbc.ap()[:, oc * 512:(oc + 1) * 512]))
                elif step == 3:
                    em("vector", lambda v, oc=oc, par=par: v.tensor_add(
                        out=asmB.ap()[:, oc, :], in0=asmA.ap()[:, oc, :],
                        in1=xblk.ap()[:, par, oc * 512:(oc + 1) * 512]))
                else:
                    oslot = 2 * par + oc
                    s_as.inc()
                    em("vector", lambda v, oc=oc, par=par, oslot=oslot: v.tensor_add(
                        out=outb.ap()[:, oslot, :], in0=asmB.ap()[:, oc, :],
                        in1=ffn_rel.ap()[:, 2 * par + oc, :]).then_inc(s_as.h, 1))
        sch = (s_st, s_s1)[par]
        for oc in range(2):
            oslot = 2 * par + oc
            wait("scalar", s_as, s_as.v - (1 - oc))
            sch.inc(16)
            em("scalar", lambda s, nb=nb, oc=oc, oslot=oslot, sch=sch: s.dma_start(
                out=out_d[nb * 128:(nb + 1) * 128, oc * 512:(oc + 1) * 512],
                in_=outb.ap()[:, oslot, :]).then_inc(sch.h, 16))

    wait("scalar", s_st, s_st.v)
    wait("scalar", s_s1, s_s1.v)

    with nc.Block() as block:
        @block.sync
        def _(q):
            for f in ops["sync"]:
                f(q)

        @block.gpsimd
        def _(g):
            for f in ops["gpsimd"]:
                f(g)

        @block.tensor
        def _(t):
            for f in ops["tensor"]:
                f(t)

        @block.scalar
        def _(s):
            for f in ops["scalar"]:
                f(s)

        @block.vector
        def _(v):
            for f in ops["vector"]:
                f(v)

    es.close()
    return nc


def _prep_weights(w_qkv, w_out, b_out, ln_g, ln_b, Wg, bg, We, be):
    w_qkv = np.asarray(w_qkv, np.float32)
    w_out = np.asarray(w_out, np.float32)
    wqk = np.empty((8, 8, 128, 256), np.float32)
    for hg in range(8):
        cols = np.concatenate([w_qkv[:, hg * 128:(hg + 1) * 128],
                               w_qkv[:, 1024 + hg * 128:1024 + (hg + 1) * 128]], axis=1)
        wqk[hg] = cols.reshape(8, 128, 256)
    wv = np.ascontiguousarray(
        w_qkv[:, 2048:3072].reshape(8, 128, 4, 256).transpose(2, 0, 1, 3))
    wout = np.ascontiguousarray(w_out.reshape(8, 128, 2, 512).transpose(2, 0, 1, 3))
    wet = np.ascontiguousarray(
        np.asarray(We, np.float32).transpose(0, 2, 1).reshape(8, 8, 128, 2, 512).transpose(0, 3, 1, 2, 4))
    wg = np.ascontiguousarray(np.asarray(Wg, np.float32).reshape(8, 128, 8))
    blko = np.zeros((128, 4), BF)
    for h in range(4):
        blko[h * 32:(h + 1) * 32, h] = BF(1.0)
    return dict(
        wqk=np.ascontiguousarray(wqk), wv=wv, wout=wout, wet=wet, wg=wg,
        bout=np.ascontiguousarray(np.asarray(b_out, np.float32).reshape(1, 1024)),
        bg=np.ascontiguousarray(np.asarray(bg, np.float32).reshape(1, 8)),
        be=np.ascontiguousarray(np.asarray(be, np.float32)),
        gam=np.ascontiguousarray(np.asarray(ln_g, np.float32).reshape(1, 1024)),
        bet=np.ascontiguousarray(np.asarray(ln_b, np.float32).reshape(1, 1024)),
        ident_hbm=np.eye(128, dtype=BF),
        blko_hbm=blko,
    )


def kernel(L0_attn_x, L0_attn_w_qkv, L0_attn_w_out, L0_attn_b_out, L0_attn_ln_gamma,
           L0_attn_ln_beta, L0_ffn_Wg, L0_ffn_bg, L0_ffn_We, L0_ffn_be):
    x = np.ascontiguousarray(np.asarray(L0_attn_x, np.float32))
    w = _prep_weights(L0_attn_w_qkv, L0_attn_w_out, L0_attn_b_out, L0_attn_ln_gamma,
                      L0_attn_ln_beta, L0_ffn_Wg, L0_ffn_bg, L0_ffn_We, L0_ffn_be)
    nc = build_nc()
    in_maps = [dict(x=np.ascontiguousarray(x[b]), **w) for b in range(8)]
    res = run_bass_kernel_spmd(nc, in_maps, core_ids=list(range(8)))
    out = np.stack([r["out"] for r in res.results]).astype(np.float32)
    return out


if __name__ == "__main__":
    rng = np.random.default_rng(0)
    ins = dict(
        L0_attn_x=rng.standard_normal((8, 2048, 1024)).astype(np.float32),
        L0_attn_w_qkv=(rng.standard_normal((1024, 3072)) * 1024 ** -0.5).astype(np.float32),
        L0_attn_w_out=(rng.standard_normal((1024, 1024)) * 1024 ** -0.5).astype(np.float32),
        L0_attn_b_out=rng.standard_normal(1024).astype(np.float32) * 0.1,
        L0_attn_ln_gamma=(1 + 0.1 * rng.standard_normal(1024)).astype(np.float32),
        L0_attn_ln_beta=rng.standard_normal(1024).astype(np.float32) * 0.1,
        L0_ffn_Wg=(rng.uniform(-1, 1, (1024, 8)) / 32).astype(np.float32),
        L0_ffn_bg=rng.standard_normal(8).astype(np.float32) * 0.01,
        L0_ffn_We=(rng.uniform(-1, 1, (8, 1024, 1024)) / 32).astype(np.float32),
        L0_ffn_be=(rng.uniform(-1, 1, (8, 1024)) / 32).astype(np.float32),
    )
    out = kernel(**ins)
    print("out", out.shape, out.dtype, np.abs(out).mean())
